# version 15
# baseline (speedup 1.0000x reference)
import numpy as np
import ml_dtypes

import concourse.bass as bass
import concourse.mybir as mybir
from concourse.bass_utils import run_bass_kernel_spmd

F32 = mybir.dt.float32
BF16 = mybir.dt.bfloat16
AF = mybir.ActivationFunctionType
ALU = mybir.AluOpType
AX = mybir.AxisListType

D = 1024
NCH = 8
HEADS = 16
DH = 64
WIN = 256
DI = 2730
NJ = 22
TILE = 512
BF_NP = ml_dtypes.bfloat16


def _sigma_perm():
    s = np.zeros(64, dtype=np.int64)
    for j in range(16):
        s[2 * j] = j;         s[2 * j + 1] = 32 + j
        s[32 + 2 * j] = 16 + j; s[32 + 2 * j + 1] = 48 + j
    return s

SIG = _sigma_perm()
SHUF32 = sum([[2 * j + 1, 2 * j] for j in range(16)], [])
SHUF_FULL = np.concatenate([32 * q + np.array(SHUF32) for q in range(4)])

def _colperm_qkv():
    perm = np.arange(3 * D)
    for base in (0, D):
        for h in range(HEADS):
            cols = base + h * DH + np.arange(DH)
            perm[base + h * DH: base + h * DH + DH] = cols[SIG]
    return perm

def _rope_tables(T):
    half = np.arange(0, DH, 2, dtype=np.float64)
    inv_freq = 1.0 / (10000.0 ** (half / DH))
    t = np.arange(T, dtype=np.float64)
    freqs = np.concatenate([t[:, None] * inv_freq[None, :]] * 2, axis=1)
    base_scale = (half + 0.4 * DH) / (1.4 * DH)
    power = (t - T // 2) / (WIN // 2)
    scale = np.concatenate([base_scale[None, :] ** power[:, None]] * 2, axis=1)
    cos, sin = np.cos(freqs), np.sin(freqs)
    sgn = np.where(np.arange(DH) < 32, -1.0, 1.0)
    lay = lambda v: v.T[SIG]
    Aq = lay(cos * scale); Bq = lay(sin * scale) * sgn[SIG][:, None]
    Ak = lay(cos / scale); Bk = lay(sin / scale) * sgn[SIG][:, None]
    dup = lambda m: np.concatenate([m, m], axis=0).astype(np.float32)
    return np.stack([dup(Aq), dup(Bq), dup(Ak), dup(Bk)])

def _masks():
    i = np.arange(128)[:, None]
    c = np.arange(384)[None, :]
    band = (i <= c) & (c <= i + WIN)
    m = np.where(band, 0.0, -1e30)
    m1 = np.where(band & (c >= 128), 0.0, -1e30)
    m0 = np.where(band & (c >= 256), 0.0, -1e30)
    return np.stack([m, m1, m0]).astype(np.float32)

def _sel_ss():
    sel = np.zeros((16, 128, 32), dtype=np.float32)
    for m in range(16):
        sel[m, 0:64, 2 * m] = 1.0
        sel[m, 64:128, 2 * m + 1] = 1.0
    return sel

def _sel2k():
    sel = np.zeros((8, 32, 128), dtype=np.float32)
    for c in range(8):
        sel[c, 16 + 2 * c, 0:64] = 1.0
        sel[c, 16 + 2 * c + 1, 64:128] = 1.0
    return sel

def _sel_stat():
    sel = np.zeros((2, 128, 2), dtype=np.float32)
    sel[0, :, 0] = 1.0
    sel[1, :, 1] = 1.0
    return sel



ENGS = ("tensor", "vector", "scalar", "gpsimd", "sync")

class It:
    __slots__ = ("sem", "cnt")
    def __init__(self, sem, cnt):
        self.sem = sem; self.cnt = cnt

class TBuf:
    __slots__ = ("w", "rs")
    def __init__(self):
        self.w = None; self.rs = []

class _MMGroup:
    def __init__(self, em):
        self.em = em; self.r = []
    def mm(self, fn, r=(), deps=(), first_w=None):
        dd = list(deps)
        for b in r:
            dd.append(b.w)
        if first_w is not None:
            dd.extend(first_w.rs); dd.append(first_w.w)
        self.em.op("tensor", fn, dd, inc=False)
        self.r.extend(r)
    def close(self, fn, r=(), w=(), deps=()):
        dd = list(deps)
        for b in r:
            dd.append(b.w)
        for b in w:
            dd.extend(b.rs); dd.append(b.w)
        it = self.em.op("tensor", fn, dd, inc=True)
        self.em.touch(it, r=self.r + list(r), w=w)
        return it


class Em:
    def __init__(self, nc, sems):
        self.nc = nc
        self.sems = sems
        self.cnt = {k: 0 for k in sems}
        self.ops = {e: [] for e in ENGS}
        self.marks = {e: {k: 0 for k in sems} for e in ENGS}
        self.nblk = 0
        self.pending_barrier = None

    def _waits(self, eng, deps):
        need = {}
        for d in deps:
            if d is None:
                continue
            if isinstance(d, (list, tuple)):
                for dd in d:
                    if dd is not None and dd.cnt > need.get(dd.sem, 0):
                        need[dd.sem] = dd.cnt
            elif d.cnt > need.get(d.sem, 0):
                need[d.sem] = d.cnt
        w = []
        for s, n in sorted(need.items()):
            if not s.startswith("c_"):
                if s in ("d_init", "d_out", "dvec"):
                    n = self.cnt[s]
                else:
                    assert n == self.cnt[s], (
                        f"unsound DMA wait on {s}: want {n}, issued {self.cnt[s]}")
            if self.marks[eng][s] >= n:
                continue
            w.append((s, n))
            self.marks[eng][s] = n
        return w

    def op(self, eng, fn, deps=(), inc=True, dma=None, nobar=False):
        sem = dma if dma else ("c_" + eng)
        if self.pending_barrier and not nobar:
            deps = list(deps) + self.pending_barrier
        waits = self._waits(eng, deps)
        if inc:
            amt = 16 if dma else 1
            self.cnt[sem] += amt
            item = It(sem, self.cnt[sem])
            self.ops[eng].append((fn, waits, (sem, amt)))
            return item
        self.ops[eng].append((fn, waits, None))
        return None

    def o(self, eng, fn, r=(), w=(), deps=(), inc=True, dma=None, nobar=False):
        dd = list(deps)
        for b in r:
            dd.append(b.w)
        for b in w:
            dd.extend(b.rs); dd.append(b.w)
        it = self.op(eng, fn, dd, inc=inc, dma=dma, nobar=nobar)
        if it is not None:
            for b in r:
                b.rs.append(it)
            for b in w:
                b.w = it; b.rs = []
        return it

    def grp(self):
        return _MMGroup(self)

    def phase_barrier(self):
        dma_items = [It(s, n) for s, n in self.cnt.items()
                     if n > 0 and not s.startswith("c_")]
        fence = self.op("sync", lambda e: e.nop(), deps=dma_items)
        self.pending_barrier = [It(s, n) for s, n in self.cnt.items()
                                if n > 0 and s.startswith("c_")] + [fence]

    def touch(self, it, r=(), w=()):
        for b in r:
            b.rs.append(it)
        for b in w:
            b.w = it; b.rs = []

    def flush(self):
        if not any(self.ops.values()):
            return
        self.nblk += 1
        with self.nc.Block(f"blk{self.nblk}") as block:
            for e in ENGS:
                lst = self.ops[e]
                if not lst:
                    continue
                def emit(eng, lst=lst):
                    for fn, waits, inc in lst:
                        for s, n in waits:
                            eng.wait_ge(self.sems[s], n)
                        ins = fn(eng)
                        if inc is not None:
                            ins.then_inc(self.sems[inc[0]], inc[1])
                getattr(block, e)(emit)
        self.ops = {e: [] for e in ENGS}



def build_program(T=4096, NL=6, NE=2):
    NT = T // TILE
    nc = bass.Bass("TRN2", target_bir_lowering=False)

    ap = lambda h: h.ap()
    x0 = ap(nc.dram_tensor("x0", [NE, D, T], F32, kind="ExternalInput"))
    wq = ap(nc.dram_tensor("wq", [D, D], F32, kind="ExternalInput"))
    wk = ap(nc.dram_tensor("wk", [D, D], F32, kind="ExternalInput"))
    wv = ap(nc.dram_tensor("wv", [D, D], F32, kind="ExternalInput"))
    bqv = ap(nc.dram_tensor("bqv", [3, D], F32, kind="ExternalInput"))
    wqkv = ap(nc.dram_tensor("wqkv", [NL, D, 3 * D], F32, kind="ExternalInput"))
    wo = ap(nc.dram_tensor("wo", [NL, D, D], F32, kind="ExternalInput"))
    w1 = ap(nc.dram_tensor("w1", [NL, D, 2 * DI], F32, kind="ExternalInput"))
    b1 = ap(nc.dram_tensor("b1", [NL, 2 * DI], F32, kind="ExternalInput"))
    w2 = ap(nc.dram_tensor("w2", [NL, DI, D], F32, kind="ExternalInput"))
    b2 = ap(nc.dram_tensor("b2", [NL, D], F32, kind="ExternalInput"))
    lng = ap(nc.dram_tensor("lng", [NL, D], F32, kind="ExternalInput"))
    lnb = ap(nc.dram_tensor("lnb", [NL, D], F32, kind="ExternalInput"))
    qsks = ap(nc.dram_tensor("qsks", [NL, 4, 128], F32, kind="ExternalInput"))
    tabs_in = ap(nc.dram_tensor("tabs", [4, 128, T], BF16, kind="ExternalInput"))
    identf_in = ap(nc.dram_tensor("identf", [128, 128], F32, kind="ExternalInput"))
    masks_in = ap(nc.dram_tensor("masks", [3, 128, 384], BF16, kind="ExternalInput"))
    bias32_in = ap(nc.dram_tensor("bias32", [32, 2], F32, kind="ExternalInput"))
    selss_in = ap(nc.dram_tensor("selss", [16, 128, 32], BF16, kind="ExternalInput"))
    sel2k_in = ap(nc.dram_tensor("sel2k", [8, 32, 128], BF16, kind="ExternalInput"))
    selst_in = ap(nc.dram_tensor("selst", [2, 128, 2], BF16, kind="ExternalInput"))

    yout = ap(nc.dram_tensor("yout", [NE, D, T], F32, kind="ExternalOutput"))

    x_dram = ap(nc.dram_tensor("x_scr", [NE, D, T + 1], F32, kind="Internal"))
    y_dram = ap(nc.dram_tensor("y_scr", [NE, D, T + 1], F32, kind="Internal"))

    xdB = [[TBuf() for _ in range(NT)] for _ in range(NE)]
    ydB = [[TBuf() for _ in range(NT)] for _ in range(NE)]

    import contextlib
    ctx = contextlib.ExitStack()

    sems = {}
    semnames = ["c_tensor", "c_vector", "c_scalar", "c_gpsimd", "c_sync",
                "d_init", "d_out", "dxa", "dxb", "dsg0", "dsg1",
                "dtab0", "dtab1", "dvec"]
    for st_ in ("x", "y"):
        for e_ in range(NE):
            semnames += [f"dst_{st_}{e_}_{i}" for i in range(NT)]
    for s in semnames:
        sems[s] = ctx.enter_context(nc.semaphore(s))
    em = Em(nc, sems)

    sb = lambda name, shape, dt: ctx.enter_context(nc.sbuf_tensor(name, shape, dt))
    ps = lambda name, shape, dt=F32: ctx.enter_context(nc.psum_tensor(name, shape, dt))

    xt = [sb(f"xt{i}", [128, NCH, TILE + 1], F32) for i in range(2)]
    xs = sb("xs", [128, NCH, TILE + 1], BF16)
    identf = sb("s_identf", [128, 128], F32)
    identb = sb("s_identb", [128, 128], BF16)
    masks = sb("s_masks", [128, 3, 384], BF16)
    bias32 = sb("s_bias32", [32, 2], F32)
    selss = sb("s_selss", [128, 16, 32], BF16)
    sel2k = sb("s_sel2k", [32, 8, 128], BF16)
    selst = sb("s_selst", [128, 2, 2], BF16)
    onesrow = sb("onesrow", [1, 512], BF16)
    onescol = sb("onescol", [128, 1], BF16)
    r1rows = sb("r1rows", [2, TILE], BF16)
    strows = sb("strows", [1, TILE * 2], F32)
    srow = lambda i: strows[:, [0, 0, 1, 0, 0][i] * TILE:([0, 0, 1, 0, 0][i] + 1) * TILE]
    rbf_row = sb("rbf_row", [1, TILE], BF16)
    rstd_sb = sb("rstd_sb", [32, TILE], F32)
    rstdT = sb("rstdT", [128, 256], F32)
    scr_sq = [sb(f"scr_sq{i}", [128, TILE], BF16) for i in range(2)]
    scr_f = [sb(f"scr_f{i}", [128, TILE], F32) for i in range(2)]
    l_sb = sb("l_sb", [128, 8], F32)
    linv_sb = sb("linv_sb", [128, 8], F32)
    nm_sb = sb("nm_sb", [128, 8], F32)
    qs_col = sb("qs_col", [128, 4], F32)
    cvals = sb("cvals", [128, 3], F32)
    halo_sb = sb("halo_sb", [128, NCH], F32)

    B = {}
    for name, t in [("xt0", xt[0]), ("xt1", xt[1]), ("xs", xs), ("identf", identf),
                    ("masks", masks), ("bias32", bias32), ("selss", selss),
                    ("sel2k", sel2k), ("selst", selst), ("onesrow", onesrow),
                    ("onescol", onescol), ("r1rows", r1rows),
                    ("strows", strows), ("rbf_row", rbf_row), ("rstd_sb", rstd_sb),
                    ("rstdT", rstdT), ("scr_sq0", scr_sq[0]), ("scr_sq1", scr_sq[1]),
                    ("scr_f0", scr_f[0]), ("scr_f1", scr_f[1]),
                    ("l_sb", l_sb), ("linv_sb", linv_sb), ("nm_sb", nm_sb),
                    ("qs_col", qs_col), ("cvals", cvals), ("halo_sb", halo_sb),
                    ("identb", identb)]:
        B[name] = TBuf()
    xtB = [B["xt0"], B["xt1"]]

    pb = [ps(f"pb{i}", [128, 512], F32) for i in range(8)]
    pbB = [TBuf() for _ in range(8)]

    zero_b = sb("zero_b", [128, TILE], BF16)
    B["zero_b"] = TBuf()

    def init_phase():
        ld = []
        ld.append(em.o("sync", lambda e: e.dma_start(out=identf[:], in_=identf_in),
                       w=[B["identf"]], dma="d_init"))
        em.o("sync", lambda e: e.dma_start(
            out=masks[:], in_=masks_in.rearrange("m p k -> p m k")),
            w=[B["masks"]], dma="d_init")
        em.o("sync", lambda e: e.dma_start(out=bias32[:], in_=bias32_in),
             w=[B["bias32"]], dma="d_init")
        em.o("sync", lambda e: e.dma_start(out=selss[:], in_=selss_in.rearrange("m p k -> p m k")),
             w=[B["selss"]], dma="d_init")
        em.o("sync", lambda e: e.dma_start(out=sel2k[:], in_=sel2k_in.rearrange("c p k -> p c k")),
             w=[B["sel2k"]], dma="d_init")
        em.o("sync", lambda e: e.dma_start(out=selst[:], in_=selst_in.rearrange("m p k -> p m k")),
             w=[B["selst"]], dma="d_init")
        em.o("vector", lambda e: e.memset(onesrow[:], 1.0), w=[B["onesrow"]])
        em.o("vector", lambda e: e.tensor_copy(out=identb[:], in_=identf[:]),
             r=[B["identf"]], w=[B["identb"]])
        em.o("vector", lambda e: e.memset(cvals[:, 0:1], 1e-5), w=[B["cvals"]])
        em.o("vector", lambda e: e.memset(cvals[:, 1:2], 1e-6 / T), w=[B["cvals"]])
        em.o("vector", lambda e: e.memset(cvals[:, 2:3], 1e-30), w=[B["cvals"]])
        em.o("vector", lambda e: e.memset(onescol[:], 1.0), w=[B["onescol"]])
        em.o("vector", lambda e: e.memset(r1rows[:], 1.0), w=[B["r1rows"]])
        em.o("vector", lambda e: e.memset(scr_f[0][:, 0:NCH], 0.0), w=[B["scr_f0"]])
        for e_ in range(NE):
            for dram, dbuf in ((x_dram, xdB), (y_dram, ydB)):
                em.o("sync", lambda eng, e_=e_, dram=dram: eng.dma_start(
                    out=dram[e_, :, 0:1].rearrange("(c p) t -> p c t", p=128),
                    in_=scr_f[0][:, 0:NCH].rearrange("p (c t) -> p c t", t=1),
                    allow_slow_non_contiguous=True),
                    r=[B["scr_f0"]], w=[dbuf[e_][0]], dma="d_init")

    def emit_stats(src_get, b1, b2):
        bankA, pA = pb[b1], pbB[b1]
        bankB, pB_ = pb[b2], pbB[b2]
        for c in range(NCH):
            src, rb = src_get(c)
            sq, sqB = scr_sq[c % 2], B[f"scr_sq{c % 2}"]
            em.o("scalar", lambda e, src=src, sq=sq: e.activation(
                out=sq[:, 0:TILE], in_=src, func=AF.Square), r=rb, w=[sqB])
            itA = em.o("tensor", lambda e, src=src, c=c: e.matmul(
                bankA[0:1, 0:TILE], onescol[:, 0:1], src,
                start=(c == 0), stop=(c == NCH - 1)),
                r=rb + [B["onescol"]],
                deps=(pA.rs + [pA.w] if c == 0 else []))
            itB = em.o("tensor", lambda e, sq=sq, c=c: e.matmul(
                bankB[0:1, 0:TILE], onescol[:, 0:1], sq[:, 0:TILE],
                start=(c == 0), stop=(c == NCH - 1)),
                r=[sqB, B["onescol"]],
                deps=(pB_.rs + [pB_.w] if c == 0 else []))
            if c == NCH - 1:
                em.touch(itA, w=[pA])
                em.touch(itB, w=[pB_])
        em.o("scalar", lambda e: e.activation(out=srow(1), in_=bankA[0:1, 0:TILE],
                                              func=AF.Square, scale=1.0 / D),
             r=[pA], w=[B["strows"]])
        em.o("vector", lambda e: e.scalar_tensor_tensor(
            out=srow(1), in0=bankB[0:1, 0:TILE], scalar=1.0 / D, in1=srow(1),
            op0=ALU.mult, op1=ALU.subtract), r=[pB_, B["strows"]], w=[B["strows"]])
        em.o("scalar", lambda e: e.activation(out=srow(1), in_=srow(1), func=AF.Ln,
                                              bias=cvals[0:1, 0:1]),
             r=[B["strows"], B["cvals"]], w=[B["strows"]])
        em.o("scalar", lambda e: e.activation(out=srow(2), in_=srow(1), func=AF.Exp,
                                              scale=-0.5), r=[B["strows"]], w=[B["strows"]])
        em.o("vector", lambda e: e.scalar_tensor_tensor(
            out=srow(1), in0=bankA[0:1, 0:TILE], scalar=1.0 / D, in1=srow(2),
            op0=ALU.mult, op1=ALU.mult), r=[pA, B["strows"]], w=[B["strows"]])
        em.o("vector", lambda e: e.tensor_scalar_mul(r1rows[0:1, :], srow(1), -1.0),
             r=[B["strows"]], w=[B["r1rows"]])
        em.o("vector", lambda e: e.tensor_copy(out=rbf_row[:], in_=srow(2)),
             r=[B["strows"]], w=[B["rbf_row"]])

    def bcast_row(src_ap, src_bufs, bank_i):
        bank, pB = pb[bank_i], pbB[bank_i]
        it = em.o("tensor", lambda e: e.matmul(bank[:, 0:TILE], onesrow[0:1, 0:128],
                                               src_ap, start=True, stop=True),
                  r=[B["onesrow"]] + list(src_bufs), w=[pB])
        return bank, pB

    def stream_convert(dst, dstB, src_ap, kparts, cols, stage, stageB, tag,
                       scale=None, pp_scalar=None, post=None, dma_eng="gpsimd"):
        for kc in range(kparts):
            for c0 in range(0, cols, 512):
                c1 = min(cols, c0 + 512)
                st = stage[kc % 2]
                stB = stageB[kc % 2]
                em.o(dma_eng, lambda e, kc=kc, c0=c0, c1=c1, st=st: e.dma_start(
                    out=st[:, 0:c1 - c0], in_=src_ap[kc * 128:(kc + 1) * 128, c0:c1]),
                    w=[stB], dma=f"dsg{kc % 2}")
                if pp_scalar is None:
                    itc = em.o("vector", lambda e, kc=kc, c0=c0, c1=c1, st=st: e.tensor_copy(
                        out=dst[:, kc, c0:c1], in_=st[:, 0:c1 - c0]),
                        r=[stB], w=[dstB[(kc, c0 // 512)]])
                else:
                    sc_ap, sc_bufs = pp_scalar(kc)
                    em.o("vector", lambda e, kc=kc, c0=c0, c1=c1, st=st, sc_ap=sc_ap:
                         e.tensor_scalar_mul(out=dst[:, kc, c0:c1], in0=st[:, 0:c1 - c0],
                                             scalar1=sc_ap),
                         r=[stB] + list(sc_bufs), w=[dstB[(kc, c0 // 512)]])
                if post is not None:
                    post(kc, c0, c1)

    def outer_block():
        with contextlib.ExitStack() as octx:
            osb = lambda name, shape, dt: octx.enter_context(nc.sbuf_tensor(name, shape, dt))
            wk_sb = osb("wk_sb", [128, NCH, D], BF16)
            wv_sb = osb("wv_sb", [128, NCH, D], BF16)
            wq_sb = osb("wq_sb", [128, NCH, D], BF16)
            vec_k = osb("vec_k", [2, D], BF16)
            vec_v = osb("vec_v", [2, D], BF16)
            vec_q = osb("vec_q", [2, D], BF16)
            kv_sb = osb("kv_sb", [128, NCH, 128], BF16)
            kst_sb = osb("kst_sb", [128, NCH], BF16)
            ktok = osb("ktok", [128, D], BF16)
            vtok = osb("vtok", [128, D], BF16)
            q_sb = osb("q_sb", [128, NCH, TILE], BF16)
            yt = osb("yt", [128, NCH, TILE], F32)
            zrow = osb("zrow", [1, 2 * TILE], F32)
            zrow_b = osb("zrow_b", [1, 2 * TILE], BF16)
            oelu = osb("oelu", [128, 2, TILE], F32)
            B["oelu0"] = TBuf(); B["oelu1"] = TBuf()
            stage = [osb(f"ostage{i}", [128, 512], F32) for i in range(2)]
            vecrow = osb("vecrow", [1, 1024], BF16)
            B["vecrow"] = TBuf()
            for nm in ("wk_sb", "wv_sb", "wq_sb", "vec_k", "vec_v", "vec_q",
                       "kst_sb", "ktok", "vtok", "q_sb", "yt", "zrow", "zrow_b"):
                B[nm] = TBuf()
            kvB = {}
            for h in range(NCH):
                kvB[h] = TBuf()
            stageB = [TBuf(), TBuf()]
            wkB = {}; wvB = {}; wqB = {}
            for kc in range(NCH):
                for g in range(2):
                    wkB[(kc, g)] = TBuf(); wvB[(kc, g)] = TBuf(); wqB[(kc, g)] = TBuf()

            def wconvert(dst, dstB, src, vec, vecB, bias_row, scale):
                stream_convert(dst, dstB, src, NCH, D, stage, stageB, "w",
                               pp_scalar=None if scale is None else None)
                if scale is not None:
                    pass
                for half in range(2):
                    g = em.grp()
                    for kc in range(NCH):
                        fn = lambda e, kc=kc, half=half: e.matmul(
                            pb[0][0:1, 0:512], onescol[:, 0:1],
                            dst[:, kc, half * 512:(half + 1) * 512],
                            start=(kc == 0), stop=(kc == NCH - 1))
                        if kc < NCH - 1:
                            g.mm(fn, r=[dstB[(kc, half)], B["onescol"]],
                                 first_w=(pbB[0] if kc == 0 else None))
                        else:
                            g.close(fn, r=[dstB[(kc, half)], B["onescol"]], w=[pbB[0]])
                    em.o("vector", lambda e, half=half: e.tensor_copy(
                        out=vec[0:1, half * 512:(half + 1) * 512],
                        in_=pb[0][0:1, 0:512]), r=[pbB[0]], w=[vecB])
                em.o("gpsimd", lambda e: e.dma_start(out=stage[0][0:1, 0:512],
                                                     in_=bias_row[0:1, 0:512]),
                     w=[stageB[0]], dma="dsg0")
                em.o("gpsimd", lambda e: e.dma_start(out=stage[1][0:1, 0:512],
                                                     in_=bias_row[0:1, 512:1024]),
                     w=[stageB[1]], dma="dsg1")
                sc = 1.0 if scale is None else scale
                em.o("vector", lambda e, sc=sc: e.tensor_scalar_mul(
                    out=vecrow[0:1, 0:512], in0=stage[0][0:1, 0:512], scalar1=sc),
                    r=[stageB[0]], w=[B["vecrow"]])
                em.o("vector", lambda e, sc=sc: e.tensor_scalar_mul(
                    out=vecrow[0:1, 512:1024], in0=stage[1][0:1, 0:512], scalar1=sc),
                    r=[stageB[1]], w=[B["vecrow"]])
                em.o("gpsimd", lambda e: e.dma_start(out=vec[1:2, 0:1024],
                                                     in_=vecrow[0:1, 0:1024]),
                     r=[B["vecrow"]], w=[vecB], dma="dvec")

            wconvert(wk_sb, wkB, wk, vec_k, B["vec_k"], bqv[1:2, :], None)
            wconvert(wv_sb, wvB, wv, vec_v, B["vec_v"], bqv[2:3, :], 1.0 / T)
            wconvert(wq_sb, wqB, wq, vec_q, B["vec_q"], bqv[0:1, :], None)
            for kc in range(NCH):
                for g in range(2):
                    em.o("vector", lambda e, kc=kc, g=g: e.tensor_scalar_mul(
                        out=wv_sb[:, kc, g * 512:(g + 1) * 512],
                        in0=wv_sb[:, kc, g * 512:(g + 1) * 512], scalar1=1.0 / T),
                        r=[wvB[(kc, g)]], w=[wvB[(kc, g)]])
            em.o("vector", lambda e: e.tensor_scalar_mul(out=vec_v[0:1, :],
                                                         in0=vec_v[0:1, :], scalar1=1.0 / T),
                 r=[B["vec_v"]], w=[B["vec_v"]])

            for e_ in range(NE):
                outer_pass1(e_, wk_sb, wkB, wv_sb, wvB, vec_k, vec_v,
                            kv_sb, kvB, kst_sb, ktok, vtok, stage, stageB, oelu)
                outer_pass2(e_, wq_sb, wqB, vec_q, kv_sb, kvB, kst_sb,
                            q_sb, yt, zrow, zrow_b, oelu)
            em.phase_barrier()
            em.flush()

    def tload(e_, ti, slot, dram, dramB, halo=True, nobar=False):
        t0 = ti * TILE
        dst = xt[slot]
        tB = xtB[slot]
        src = dram[e_, :, t0 + 1:t0 + TILE + 1].rearrange("(c p) t -> p c t", p=128)
        dep = dramB[e_][ti]
        it = em.o("sync", lambda e: e.dma_start(out=dst[:, :, 1:TILE + 1], in_=src),
                  w=[tB], deps=[dep.w], dma=("dxa" if slot == 0 else "dxb"),
                  nobar=nobar)
        dep.rs.append(it)
        if halo:
            if ti == 0:
                em.o("vector", lambda e: e.memset(halo_sb[:], 0.0),
                     w=[B["halo_sb"]], nobar=nobar)
            em.o("vector", lambda e: e.tensor_copy(
                out=dst[:, :, 0:1], in_=halo_sb[:].rearrange("p (c o) -> p c o", o=1)),
                r=[B["halo_sb"]], w=[tB], nobar=nobar)
            em.o("vector", lambda e: e.tensor_copy(
                out=halo_sb[:].rearrange("p (c o) -> p c o", o=1),
                in_=dst[:, :, TILE:TILE + 1]), r=[tB], w=[B["halo_sb"]], nobar=nobar)
        else:
            em.o("vector", lambda e: e.memset(dst[:, :, 0:1], 0.0), w=[tB],
                 nobar=nobar)
        return dst, tB

    def tstore(e_, ti, slot, dram, dramB, src=None, srcB=None):
        t0 = ti * TILE
        s = xt[slot] if src is None else src
        sB = xtB[slot] if srcB is None else srcB
        dstap = dram[e_, :, t0 + 1:t0 + TILE + 1].rearrange("(c p) t -> p c t", p=128)
        sl = s[:, :, 1:TILE + 1] if src is None else s[:, :, 0:TILE]
        sname = "x" if dram is x_dram else "y"
        it = em.o("gpsimd", lambda e: e.dma_start(out=dstap, in_=sl),
                  r=[sB], w=[dramB[e_][ti]], dma=f"dst_{sname}{e_}_{ti}")
        return it

    def cast_xs(xtile, xtB_, scaled_bank=None, scaled_pB=None, shifted=False,
                halo=False):
        its = []
        if scaled_bank is None:
            sl = slice(0, TILE + 1) if halo else slice(1, TILE + 1)
            for c in range(NCH):
                its.append(em.o("vector", lambda e, c=c, sl=sl: e.tensor_copy(
                    out=xs[:, c, sl], in_=xtile[:, c, sl]), r=[xtB_], w=[B["xs"]]))
        else:
            for c in range(NCH):
                sl = slice(1, TILE + 1) if (not shifted or c < 4) else slice(0, TILE)
                its.append(em.o("vector", lambda e, c=c, sl=sl: e.tensor_tensor(
                    out=xs[:, c, 0:TILE], in0=xtile[:, c, sl],
                    in1=scaled_bank[:, 0:TILE], op=ALU.mult),
                    r=[xtB_, scaled_pB], w=[B["xs"]]))
        return its

    def outer_pass1(e_, wk_sb, wkB, wv_sb, wvB, vec_k, vec_v, kv_sb, kvB,
                    kst_sb, ktok, vtok, stage, stageB, oelu):
        def oload(ti):
            slot = ti % 2
            xtile, xtB_ = xt[slot], xtB[slot]
            t0 = ti * TILE
            srcap = x0[e_, :, t0:t0 + TILE].rearrange("(c p) t -> p c t", p=128)
            em.o("sync", lambda e, srcap=srcap, xtile=xtile: e.dma_start(
                out=xtile[:, :, 1:TILE + 1], in_=srcap), w=[xtB_],
                dma=("dxa" if slot == 0 else "dxb"))

        oload(0)
        for ti in range(T // TILE):
            slot = ti % 2
            xtile, xtB_ = xt[slot], xtB[slot]
            t0 = ti * TILE
            if ti + 1 < T // TILE:
                oload(ti + 1)
            cast_xs(xtile, xtB_)
            emit_stats(lambda c: (xs[:, c, 1:TILE + 1], [B["xs"]]), 2, 3)
            em.o("vector", lambda e: e.tensor_copy(out=rbf_row[:], in_=srow(2)),
                 r=[B["strows"]], w=[B["rbf_row"]])
            rb_bank, rb_pB = bcast_row(rbf_row[0:1, :], [B["rbf_row"]], 2)
            cast_xs(xtile, xtB_, rb_bank, rb_pB)
            for ts in range(TILE // 128):
                tc = ti * 4 + ts
                for (wsb, wB, vec, vecB, dstT, dstB_) in (
                        (wk_sb, wkB, vec_k, B["vec_k"], ktok, B["ktok"]),
                        (wv_sb, wvB, vec_v, B["vec_v"], vtok, B["vtok"])):
                    for half in range(2):
                        bank, pB = pb[ts % 2], pbB[ts % 2]
                        g = em.grp()
                        for kc in range(NCH):
                            g.mm(lambda e, kc=kc, half=half, ts=ts, wsb=wsb, bank=bank:
                                 e.matmul(bank[:, 0:512],
                                          xs[:, kc, ts * 128:ts * 128 + 128],
                                          wsb[:, kc, half * 512:(half + 1) * 512],
                                          start=(kc == 0), stop=False),
                                 r=[B["xs"], wB[(kc, half)]],
                                 first_w=(pB if kc == 0 else None))
                        g.close(lambda e, half=half, vec=vec, bank=bank, ts=ts:
                                e.matmul(bank[:, 0:512],
                                         r1rows[0:2, ts * 128:ts * 128 + 128],
                                         vec[0:2, half * 512:(half + 1) * 512],
                                         start=False, stop=True),
                                r=[B["r1rows"], vecB], w=[pB])
                        if dstT is ktok:
                            ep = (ts * 2 + half) % 2
                            epB = B[f"oelu{ep}"]
                            em.o("vector", lambda e, bank=bank, ep=ep: e.tensor_scalar_min(
                                out=oelu[:, ep, :], in0=bank[:, 0:512], scalar1=0.0),
                                r=[pB], w=[epB])
                            em.o("scalar", lambda e, ep=ep: e.activation(
                                out=oelu[:, ep, :], in_=oelu[:, ep, :], func=AF.Exp),
                                r=[epB], w=[epB])
                            em.o("vector", lambda e, bank=bank, dstT=dstT, half=half, ep=ep:
                                 e.scalar_tensor_tensor(
                                     out=dstT[:, half * 512:(half + 1) * 512],
                                     in0=bank[:, 0:512], scalar=0.0, in1=oelu[:, ep, :],
                                     op0=ALU.max, op1=ALU.add),
                                 r=[pB, epB], w=[dstB_])
                        else:
                            em.o("scalar", lambda e, bank=bank, dstT=dstT, half=half:
                                 e.activation(out=dstT[:, half * 512:(half + 1) * 512],
                                              in_=bank[:, 0:512], func=AF.Copy),
                                 r=[pB], w=[dstB_])
                first = (tc == 0)
                last = (tc == T // 128 - 1)
                if first:
                    for bi in (6, 7):
                        em.o("tensor", lambda e, bi=bi: e.matmul(
                            pb[bi][:, 0:512], zero_b[0:1, 0:128],
                            zero_b[0:1, 0:512], start=True, stop=False),
                            r=[B["zero_b"]], deps=pbB[bi].rs + [pbB[bi].w],
                            inc=False)
                for h in range(8):
                    kvbank = pb[6 + h // 4]
                    kvpB = pbB[6 + h // 4]
                    mmit = em.o("tensor", lambda e, h=h, kvbank=kvbank, last=last: e.matmul(
                        kvbank[:, (h % 4) * 128:(h % 4) * 128 + 128],
                        ktok[:, h * 128:(h + 1) * 128],
                        vtok[:, h * 128:(h + 1) * 128],
                        start=False, stop=(last and (h % 4) == 3)),
                        r=[B["ktok"], B["vtok"]],
                        inc=True)
                    em.touch(mmit, w=[kvpB])
                for half in range(2):
                    ksbank = pb[5 - half]
                    kspB = pbB[5 - half]
                    mmit = em.o("tensor", lambda e, half=half, ksbank=ksbank,
                                first=first, last=last: e.matmul(
                        ksbank[0:1, 0:512], onescol[:, 0:1],
                        ktok[:, half * 512:(half + 1) * 512],
                        start=first, stop=last),
                        r=[B["ktok"], B["onescol"]],
                        deps=(kspB.rs + [kspB.w] if first else []), inc=True)
                    em.touch(mmit, w=[kspB])
        for h in range(8):
            em.o("scalar", lambda e, h=h: e.activation(
                out=kv_sb[:, h, :], in_=pb[6 + h // 4][:, (h % 4) * 128:(h % 4) * 128 + 128],
                func=AF.Copy), r=[pbB[6 + h // 4]], w=[kvB[h]])
        em.o("vector", lambda e: e.memset(kst_sb[:], 0.0), w=[B["kst_sb"]])
        for h in range(8):
            bank = pb[5 - h // 4]
            col = (h % 4) * 128
            em.o("vector", lambda e, h=h, bank=bank, col=col: e.tensor_copy(
                out=scr_f[0][0:1, 0:128], in_=bank[0:1, col:col + 128]),
                r=[pbB[5 - h // 4]], w=[B["scr_f0"]])
            mmit = em.o("tensor", lambda e: e.transpose(
                pb[0][:, 0:1], scr_f[0][0:1, 0:128], identf[0:1, 0:1]),
                r=[B["scr_f0"], B["identf"]], deps=pbB[0].rs + [pbB[0].w])
            em.touch(mmit, w=[pbB[0]])
            em.o("vector", lambda e, h=h: e.tensor_copy(
                out=kst_sb[:, h:h + 1], in_=pb[0][:, 0:1]),
                r=[pbB[0]], w=[B["kst_sb"]])

    def outer_pass2(e_, wq_sb, wqB, vec_q, kv_sb, kvB, kst_sb, q_sb, yt,
                    zrow, zrow_b, oelu):
        def oload2(ti):
            slot = ti % 2
            xtile, xtB_ = xt[slot], xtB[slot]
            t0 = ti * TILE
            srcap = x0[e_, :, t0:t0 + TILE].rearrange("(c p) t -> p c t", p=128)
            em.o("sync", lambda e, srcap=srcap, xtile=xtile: e.dma_start(
                out=xtile[:, :, 1:TILE + 1], in_=srcap), w=[xtB_],
                dma=("dxa" if slot == 0 else "dxb"))

        oload2(0)
        for ti in range(T // TILE):
            slot = ti % 2
            xtile, xtB_ = xt[slot], xtB[slot]
            t0 = ti * TILE
            if ti + 1 < T // TILE:
                oload2(ti + 1)
            cast_xs(xtile, xtB_)
            emit_stats(lambda c: (xs[:, c, 1:TILE + 1], [B["xs"]]), 2, 3)
            em.o("vector", lambda e: e.tensor_copy(out=rbf_row[:], in_=srow(2)),
                 r=[B["strows"]], w=[B["rbf_row"]])
            rb_bank, rb_pB = bcast_row(rbf_row[0:1, :], [B["rbf_row"]], 2)
            cast_xs(xtile, xtB_, rb_bank, rb_pB)
            for oc in range(NCH):
                bank, pB = pb[oc % 2], pbB[oc % 2]
                g = em.grp()
                for kc in range(NCH):
                    g.mm(lambda e, kc=kc, oc=oc, bank=bank: e.matmul(
                        bank[:, 0:TILE], wq_sb[:, kc, oc * 128:(oc + 1) * 128],
                        xs[:, kc, 0:TILE], start=(kc == 0), stop=False),
                        r=[B["xs"], wqB[(kc, oc // 4)]],
                        first_w=(pB if kc == 0 else None))
                g.close(lambda e, oc=oc, bank=bank: e.matmul(
                    bank[:, 0:TILE], vec_q[0:2, oc * 128:(oc + 1) * 128],
                    r1rows[0:2, 0:TILE], start=False, stop=True),
                    r=[B["vec_q"], B["r1rows"]], w=[pB])
                ep = oc % 2
                epB = B[f"oelu{ep}"]
                em.o("vector", lambda e, bank=bank, ep=ep: e.tensor_scalar_min(
                    out=oelu[:, ep, :], in0=bank[:, 0:TILE], scalar1=0.0),
                    r=[pB], w=[epB])
                em.o("scalar", lambda e, ep=ep: e.activation(
                    out=oelu[:, ep, :], in_=oelu[:, ep, :], func=AF.Exp),
                    r=[epB], w=[epB])
                em.o("vector", lambda e, bank=bank, oc=oc, ep=ep: e.scalar_tensor_tensor(
                    out=q_sb[:, oc, :], in0=bank[:, 0:TILE], scalar=0.0,
                    in1=oelu[:, ep, :], op0=ALU.max, op1=ALU.add),
                    r=[pB, epB], w=[B["q_sb"]])
            for h in range(NCH):
                zp = h % 2
                zr = zrow[0:1, zp * TILE:(zp + 1) * TILE]
                zrb = zrow_b[0:1, zp * TILE:(zp + 1) * TILE]
                zrB = B["zrow"] if zp == 0 else B["zrow_b"]
                sqp, sqpB = scr_sq[zp], B[f"scr_sq{zp}"]
                bank, pB = pb[4 + h % 2], pbB[4 + h % 2]
                mmit = em.o("tensor", lambda e, h=h, bank=bank: e.matmul(
                    bank[0:1, 0:TILE], kst_sb[:, h:h + 1], q_sb[:, h, :],
                    start=True, stop=True), r=[B["kst_sb"], B["q_sb"]],
                    w=[pB])
                em.o("vector", lambda e, bank=bank, zr=zr: e.tensor_scalar(
                    out=zr, in0=bank[0:1, 0:TILE], scalar1=1.0 / T,
                    scalar2=1e-6 / T, op0=ALU.mult, op1=ALU.add),
                    r=[pB], w=[zrB])
                em.o("vector", lambda e, zr=zr: e.reciprocal(out=zr, in_=zr),
                     r=[zrB], w=[zrB])
                em.o("vector", lambda e, zr=zr, zrb=zrb: e.tensor_copy(out=zrb, in_=zr),
                     r=[zrB], w=[zrB])
                zb_bank, zb_pB = bcast_row(zrb, [zrB], 6 + h % 2)
                em.o("vector", lambda e, h=h, zb_bank=zb_bank, sqp=sqp: e.tensor_tensor(
                    out=sqp[:, 0:TILE], in0=q_sb[:, h, :],
                    in1=zb_bank[:, 0:TILE], op=ALU.mult),
                    r=[B["q_sb"], zb_pB], w=[sqpB])
                mm2 = em.o("tensor", lambda e, h=h, bank=bank, sqp=sqp: e.matmul(
                    bank[:, 0:TILE], kv_sb[:, h, :], sqp[:, 0:TILE],
                    start=True, stop=True),
                    r=[kvB[h], sqpB], deps=[pB.w] + pB.rs)
                em.touch(mm2, w=[pB])
                em.o("vector", lambda e, h=h, bank=bank, xtile=xtile: e.tensor_tensor(
                    out=xtile[:, h, 1:TILE + 1], in0=xtile[:, h, 1:TILE + 1],
                    in1=bank[:, 0:TILE], op=ALU.add), r=[pB, xtB_], w=[xtB_])
            cast_xs(xtile, xtB_)
            emit_stats(lambda c: (xs[:, c, 1:TILE + 1], [B["xs"]]), 2, 3)
            em.o("vector", lambda e: e.tensor_copy(out=rbf_row[:], in_=srow(2)),
                 r=[B["strows"]], w=[B["rbf_row"]])
            rb_bank, rb_pB = bcast_row(rbf_row[0:1, :], [B["rbf_row"]], 2)
            em.o("vector", lambda e: e.tensor_copy(out=zrow_b[0:1, 0:TILE], in_=srow(3)),
                 r=[B["strows"]], w=[B["zrow_b"]])
            mub_bank, mub_pB = bcast_row(zrow_b[0:1, 0:TILE], [B["zrow_b"]], 3)
            for c in range(NCH):
                em.o("vector", lambda e, c=c, xtile=xtile: e.tensor_tensor(
                    out=scr_f[c % 2][:, 0:TILE], in0=xtile[:, c, 1:TILE + 1],
                    in1=rb_bank[:, 0:TILE], op=ALU.mult),
                    r=[xtB_, rb_pB], w=[B[f"scr_f{c % 2}"]])
                em.o("vector", lambda e, c=c: e.tensor_tensor(
                    out=yt[:, c, :], in0=scr_f[c % 2][:, 0:TILE],
                    in1=mub_bank[:, 0:TILE], op=ALU.subtract),
                    r=[B[f"scr_f{c % 2}"], mub_pB], w=[B["yt"]])
            tstore(e_, ti, slot, x_dram, xdB)
            em.o("sync", lambda e, t0=t0, e_=e_: e.dma_start(
                out=y_dram[e_, :, t0 + 1:t0 + TILE + 1].rearrange("(c p) t -> p c t", p=128),
                in_=yt[:, :, 0:TILE]), r=[B["yt"]], w=[ydB[e_][ti]], dma=f"dst_y{e_}_{ti}")



    def inner_attn_elem(l, e_, wqkv_sb, wqkvB, wo_sb, Lb, kwin, kwinB, vring,
                        vrB, tabs_raw, tabs_pr, q_sb, p_sb, pT_sb, aout,
                        rstd_b):
        woB = Lb["wo"]

        def ring_ranges(start_slot, nslots):
            out = []
            s = start_slot % 6
            remaining, woff = nslots, 0
            while remaining:
                take = min(remaining, 6 - s)
                out.append((s * 128, woff * 128, take * 128))
                woff += take; remaining -= take; s = 0
            return out

        def tile_head(ti, nobar=False):
            slot = ti % 2
            xtile, xtB_ = xt[slot], xtB[slot]
            tload(e_, ti, slot, y_dram, ydB, nobar=nobar)
            cast_xs(xtile, xtB_, halo=True)
            tb, tbB = tabs_raw[ti % 2], Lb[f"traw{ti % 2}"]
            t0 = ti * TILE
            em.o("gpsimd", lambda e, t0=t0, tb=tb: e.dma_start(
                out=tb[:], in_=tabs_in[:, :, t0:t0 + TILE].rearrange("k p t -> p k t")),
                w=[tbB], dma=f"dtab{ti % 2}")

        tile_head(0, nobar=(e_ == 0))
        for ti in range(NT):
            slot = ti % 2
            par2 = ti % 2
            qbase = par2 * TILE
            qB = Lb[f"q{par2}"]
            w0 = (ti * 4) % 6
            new_ranges = ring_ranges(w0, 4)
            new_slots = [(w0 + k) % 6 for k in range(4)]
            xtile, xtB_ = xt[slot], xtB[slot]
            tb, tbB = tabs_raw[ti % 2], Lb[f"traw{ti % 2}"]
            for k4 in range(4):
                em.o("vector", lambda e, k4=k4, tb=tb: e.tensor_scalar_mul(
                    out=tabs_pr[:, k4, :], in0=tb[:, k4, :],
                    scalar1=qs_col[:, k4:k4 + 1]),
                    r=[tbB, B["qs_col"]], w=[Lb["tabs_pr"]])
            if ti == 0:
                em.o("vector", lambda e: e.memset(kwin[:, :, 512:768], 0.0),
                     w=[kwinB[4], kwinB[5]])
                for sl in (4, 5):
                    em.o("vector", lambda e, sl=sl: e.memset(vring[sl][:], 0.0),
                         w=[vrB[sl]])
            def shsl(kc, off, width):
                base = 1 if kc < 4 else 0
                return xs[:, kc, base + off: base + off + width]
            for side in ("q", "k"):
                colbase = 0 if side == "q" else D
                for oc in range(NCH):
                    bank, pB = pb[oc % 2], pbB[oc % 2]
                    g = em.grp()
                    for kc in range(NCH):
                        fn = lambda e, kc=kc, oc=oc, bank=bank, colbase=colbase: e.matmul(
                            bank[:, 0:TILE],
                            wqkv_sb[:, kc, colbase + oc * 128: colbase + (oc + 1) * 128],
                            shsl(kc, 0, TILE), start=(kc == 0), stop=(kc == NCH - 1))
                        if kc < NCH - 1:
                            g.mm(fn, r=[B["xs"], wqkvB[(kc, (colbase + oc * 128) // 512)]],
                                 first_w=(pB if kc == 0 else None))
                        else:
                            g.close(fn, r=[B["xs"], wqkvB[(kc, (colbase + oc * 128) // 512)]],
                                    w=[pB])
                    if side == "q":
                        em.o("scalar", lambda e, oc=oc, bank=bank, qbase=qbase:
                             e.activation(
                            out=q_sb[:, oc, qbase:qbase + TILE], in_=bank[:, 0:TILE],
                            func=AF.Copy), r=[pB], w=[qB])
                    else:
                        for (rc, to, wd) in new_ranges:
                            wslots = [kwinB[rc // 128 + k] for k in range(wd // 128)]
                            em.o("scalar", lambda e, oc=oc, bank=bank, rc=rc, to=to, wd=wd:
                                 e.activation(
                                out=kwin[:, oc, rc:rc + wd], in_=bank[:, to:to + wd],
                                func=AF.Copy), r=[pB], w=wslots)
            pss, pssB = pb[4], pbB[4]
            nmm = 0
            for side in ("q", "k"):
                for c in range(NCH):
                    sq, sqB = scr_sq[c % 2], B[f"scr_sq{c % 2}"]
                    first = (nmm == 0)
                    if side == "q":
                        em.o("scalar", lambda e, c=c, sq=sq, qbase=qbase: e.activation(
                            out=sq[:, 0:TILE], in_=q_sb[:, c, qbase:qbase + TILE],
                            func=AF.Square), r=[qB], w=[sqB])
                    else:
                        for (rc, to, wd) in new_ranges:
                            rslots = [kwinB[rc // 128 + k] for k in range(wd // 128)]
                            em.o("scalar", lambda e, c=c, sq=sq, rc=rc, to=to, wd=wd:
                                 e.activation(
                                out=sq[:, to:to + wd], in_=kwin[:, c, rc:rc + wd],
                                func=AF.Square),
                                r=rslots, w=[sqB])
                    last = (nmm == 15)
                    mbase = 0 if side == "q" else 8
                    it = em.o("tensor", lambda e, c=c, sq=sq, mbase=mbase,
                              first=first, last=last: e.matmul(
                        pss[0:32, 0:TILE], selss[:, mbase + c, :], sq[:, 0:TILE],
                        start=first, stop=last),
                        r=[sqB, B["selss"]],
                        deps=(pssB.rs + [pssB.w] if first else []))
                    if last:
                        em.touch(it, w=[pssB])
                    nmm += 1
            em.o("scalar", lambda e: e.activation(
                out=rstd_sb[:, 0:TILE], in_=pss[0:32, 0:TILE], func=AF.Ln,
                bias=cvals[0:32, 2:3]),
                r=[pssB, B["cvals"]], w=[B["rstd_sb"]])
            em.o("scalar", lambda e: e.activation(
                out=rstd_sb[:, 0:TILE], in_=rstd_sb[:, 0:TILE], func=AF.Exp,
                scale=-0.5, bias=bias32[:, 0:1]),
                r=[B["rstd_sb"], B["bias32"]], w=[B["rstd_sb"]])
            em.o("vector", lambda e: e.tensor_scalar(
                out=rstd_sb[:, 0:TILE], in0=rstd_sb[:, 0:TILE],
                scalar1=bias32[:, 1:2], scalar2=None, op0=ALU.min),
                r=[B["rstd_sb"], B["bias32"]], w=[B["rstd_sb"]])
            em.o("vector", lambda e: e.tensor_copy(out=rstd_b[:], in_=rstd_sb[:, 0:TILE]),
                 r=[B["rstd_sb"]], w=[Lb["rstd_b"]])
            rTB = Lb[f"rstdT{par2}"]
            for s in range(4):
                it = em.o("tensor", lambda e, s=s: e.transpose(
                    pb[5][:, 0:32], rstd_sb[:, s * 128:(s + 1) * 128],
                    identf[0:32, 0:32]), r=[B["rstd_sb"], B["identf"]],
                    deps=pbB[5].rs + [pbB[5].w])
                em.touch(it, w=[pbB[5]])
                em.o("vector", lambda e, s=s, par2=par2: e.tensor_copy(
                    out=rstdT[:, par2 * 128 + s * 32: par2 * 128 + (s + 1) * 32],
                    in_=pb[5][:, 0:32]),
                    r=[pbB[5]], w=[rTB])
            for c in range(NCH):
                em.o("vector", lambda e, c=c, qbase=qbase: e.stream_shuffle(
                    out=scr_sq[0][:, 0:TILE], in_=q_sb[:, c, qbase:qbase + TILE],
                    mask=SHUF32), r=[qB], w=[B["scr_sq0"]])
                em.o("vector", lambda e, c=c, qbase=qbase: e.tensor_tensor(
                    out=scr_sq[1][:, 0:TILE], in0=q_sb[:, c, qbase:qbase + TILE],
                    in1=tabs_pr[:, 0, :], op=ALU.mult),
                    r=[qB, Lb["tabs_pr"]], w=[B["scr_sq1"]])
                em.o("vector", lambda e: e.tensor_tensor(
                    out=scr_sq[0][:, 0:TILE], in0=scr_sq[0][:, 0:TILE],
                    in1=tabs_pr[:, 1, :], op=ALU.mult),
                    r=[B["scr_sq0"], Lb["tabs_pr"]], w=[B["scr_sq0"]])
                em.o("vector", lambda e, c=c, qbase=qbase: e.tensor_tensor(
                    out=q_sb[:, c, qbase:qbase + TILE], in0=scr_sq[1][:, 0:TILE],
                    in1=scr_sq[0][:, 0:TILE], op=ALU.add),
                    r=[B["scr_sq0"], B["scr_sq1"]], w=[qB])
                it = em.o("tensor", lambda e, c=c: e.matmul(
                    pb[5][:, 0:TILE], sel2k[:, c, :], rstd_b[:],
                    start=True, stop=True), r=[B["sel2k"], Lb["rstd_b"]],
                    deps=pbB[5].rs + [pbB[5].w])
                em.touch(it, w=[pbB[5]])
                for (rc, to, wd) in new_ranges:
                    kB_ = kwinB[rc // 128]
                    kslots = [kwinB[rc // 128 + k] for k in range(wd // 128)]
                    em.o("vector", lambda e, c=c, rc=rc, wd=wd: e.stream_shuffle(
                        out=scr_sq[0][:, 0:wd], in_=kwin[:, c, rc:rc + wd],
                        mask=SHUF32), r=kslots, w=[B["scr_sq0"]])
                    em.o("vector", lambda e, c=c, rc=rc, to=to, wd=wd: e.tensor_tensor(
                        out=scr_sq[1][:, 0:wd], in0=kwin[:, c, rc:rc + wd],
                        in1=tabs_pr[:, 2, to:to + wd], op=ALU.mult),
                        r=kslots + [Lb["tabs_pr"]], w=[B["scr_sq1"]])
                    em.o("vector", lambda e, to=to, wd=wd: e.tensor_tensor(
                        out=scr_sq[0][:, 0:wd], in0=scr_sq[0][:, 0:wd],
                        in1=tabs_pr[:, 3, to:to + wd], op=ALU.mult),
                        r=[B["scr_sq0"], Lb["tabs_pr"]], w=[B["scr_sq0"]])
                    em.o("vector", lambda e, wd=wd: e.tensor_tensor(
                        out=scr_sq[1][:, 0:wd], in0=scr_sq[1][:, 0:wd],
                        in1=scr_sq[0][:, 0:wd], op=ALU.add),
                        r=[B["scr_sq0"], B["scr_sq1"]], w=[B["scr_sq1"]])
                    em.o("vector", lambda e, c=c, rc=rc, to=to, wd=wd: e.tensor_tensor(
                        out=kwin[:, c, rc:rc + wd], in0=scr_sq[1][:, 0:wd],
                        in1=pb[5][:, to:to + wd], op=ALU.mult),
                        r=[B["scr_sq1"], pbB[5]], w=kslots)
            for ts in range(4):
                gslot = (ti * 4 + ts) % 6
                for half in range(2):
                    bank, pB = pb[2 + ts % 2], pbB[2 + ts % 2]
                    g = em.grp()
                    for kc in range(NCH):
                        fn = lambda e, kc=kc, half=half, ts=ts, bank=bank: e.matmul(
                            bank[:, 0:512], shsl(kc, ts * 128, 128),
                            wqkv_sb[:, kc, 2 * D + half * 512: 2 * D + (half + 1) * 512],
                            start=(kc == 0), stop=(kc == NCH - 1))
                        rr = [B["xs"], wqkvB[(kc, (2 * D + half * 512) // 512)]]
                        if kc < NCH - 1:
                            g.mm(fn, r=rr, first_w=(pB if kc == 0 else None))
                        else:
                            g.close(fn, r=rr, w=[pB])
                    em.o("scalar", lambda e, gslot=gslot, half=half, bank=bank:
                         e.activation(out=vring[gslot][:, half * 512:(half + 1) * 512],
                                      in_=bank[:, 0:512], func=AF.Copy),
                         r=[pB], w=[vrB[gslot]])
            if ti + 1 < NT:
                tile_head(ti + 1)
            for s in range(4):
                s_glob = ti * 4 + s
                mvar = 2 if s_glob == 0 else (1 if s_glob == 1 else 0)
                win_ranges = ring_ranges((w0 + s - 2) % 6, 3)
                win_slots = [((w0 + s - 2) + k) % 6 for k in range(3)]
                for bi in (6, 7):
                    em.o("tensor", lambda e, bi=bi: e.matmul(
                        pb[bi][:, 0:512], zero_b[0:1, 0:128], zero_b[0:1, 0:512],
                        start=True, stop=False),
                        r=[B["zero_b"]], deps=pbB[bi].rs + [pbB[bi].w], inc=False)

                def st_sims(c):
                    for hh in range(2):
                        bank, pB = pb[2 + (c % 2) * 2 + hh], pbB[2 + (c % 2) * 2 + hh]
                        nr = len(win_ranges)
                        for ri, (rc, to, wd) in enumerate(win_ranges):
                            it = em.o("tensor", lambda e, c=c, s=s, hh=hh, bank=bank,
                                      rc=rc, to=to, wd=wd, ri=ri, nr=nr, qbase=qbase:
                                      e.matmul(bank[:, to:to + wd],
                                               q_sb[hh * 64:(hh + 1) * 64, c,
                                                    qbase + s * 128:qbase + (s + 1) * 128],
                                               kwin[hh * 64:(hh + 1) * 64, c, rc:rc + wd],
                                               start=(ri == 0), stop=(ri == nr - 1)),
                                      deps=[qB.w, kwinB[rc // 128].w]
                                           + (pB.rs + [pB.w] if ri == 0 else []),
                                      inc=(ri == nr - 1))
                            if ri == nr - 1:
                                em.touch(it, w=[pB],
                                         r=[qB] + [kwinB[sl] for sl in win_slots])

                def st_mask(c):
                    for hh in range(2):
                        ix = (c % 4) * 2 + hh
                        bank, pB = pb[2 + (c % 2) * 2 + hh], pbB[2 + (c % 2) * 2 + hh]
                        em.o("vector", lambda e, c=c, s=s, ix=ix, bank=bank,
                             mvar=mvar, par2=par2: e.scalar_tensor_tensor(
                                 out=bank[:, 0:384], in0=bank[:, 0:384],
                                 scalar=rstdT[:, par2 * 128 + s * 32 + c * 2 + (ix % 2):
                                              par2 * 128 + s * 32 + c * 2 + (ix % 2) + 1],
                                 in1=masks[:, mvar, :], op0=ALU.mult, op1=ALU.add),
                             r=[rTB, B["masks"], pB], w=[pB])
                        em.o("vector", lambda e, ix=ix, bank=bank: e.reduce_max(
                            out=nm_sb[:, ix:ix + 1], in_=bank[:, 0:384],
                            axis=AX.X, negate=True),
                            r=[pB], w=[B[f"nm{ix}"]])

                def st_exp(c):
                    for hh in range(2):
                        ix = (c % 4) * 2 + hh
                        bank, pB = pb[2 + (c % 2) * 2 + hh], pbB[2 + (c % 2) * 2 + hh]
                        em.o("scalar", lambda e, ix=ix, bank=bank: e.activation(
                            out=p_sb[:, ix, :], in_=bank[:, 0:384], func=AF.Exp,
                            bias=nm_sb[:, ix:ix + 1], accum_out=l_sb[:, ix:ix + 1]),
                            r=[pB, B[f"nm{ix}"]],
                            w=[Lb[f"p{ix}"], B[f"l{ix}"]])

                def st_tail1(c):
                    for hh in range(2):
                        ix = (c % 4) * 2 + hh
                        em.o("vector", lambda e, ix=ix: e.reciprocal(
                            out=linv_sb[:, ix:ix + 1], in_=l_sb[:, ix:ix + 1]),
                            r=[B[f"l{ix}"]], w=[B[f"li{ix}"]])
                        em.o("vector", lambda e, ix=ix: e.tensor_scalar_mul(
                            out=p_sb[:, ix, :], in0=p_sb[:, ix, :],
                            scalar1=linv_sb[:, ix:ix + 1]),
                            r=[B[f"li{ix}"], Lb[f"p{ix}"]], w=[Lb[f"p{ix}"]])

                def st_tail2(c):
                    for hh in range(2):
                        h = 2 * c + hh
                        ix = (c % 4) * 2 + hh
                        tbank, tpB = pb[hh], pbB[hh]
                        tview = tbank[:].bitcast(BF16)
                        for kc3 in range(3):
                            it = em.o("tensor", lambda e, ix=ix, kc3=kc3, tview=tview:
                                      e.transpose(
                                tview[:, kc3 * 128:(kc3 + 1) * 128],
                                p_sb[:, ix, kc3 * 128:(kc3 + 1) * 128], identb[:]),
                                r=[Lb[f"p{ix}"], B["identb"]],
                                deps=(tpB.rs + [tpB.w] if kc3 == 0 else []),
                                inc=(kc3 == 2))
                            if kc3 == 2:
                                em.touch(it, w=[tpB])
                        if (c + hh) % 2 == 0:
                            em.o("scalar", lambda e, ix=ix, tview=tview: e.activation(
                                out=pT_sb[:, ix, :], in_=tview[:, 0:384], func=AF.Copy),
                                r=[tpB], w=[Lb[f"pT{ix}"]])
                        else:
                            em.o("vector", lambda e, ix=ix, tview=tview: e.tensor_copy(
                                out=pT_sb[:, ix, :], in_=tview[:, 0:384]),
                                r=[tpB], w=[Lb[f"pT{ix}"]])
                        avbank = pb[6 + c // 4]
                        avpB = pbB[6 + c // 4]
                        hp_col = (c % 4) * 128
                        base = hh * 64
                        av_slots = []
                        for kc3 in range(3):
                            gslot = (ti * 4 + s - 2 + kc3) % 6
                            av_slots.append(vrB[gslot])
                            it = em.o("tensor", lambda e, ix=ix, kc3=kc3,
                                      gslot=gslot, avbank=avbank, hp_col=hp_col,
                                      base=base, h=h: e.matmul(
                                avbank[base:base + 64, hp_col:hp_col + 128],
                                vring[gslot][:, h * 64:(h + 1) * 64],
                                pT_sb[:, ix, kc3 * 128:(kc3 + 1) * 128],
                                start=False, stop=False,
                                tile_position=(0, base)),
                                deps=[vrB[gslot].w, Lb[f"pT{ix}"].w],
                                inc=(kc3 == 2))
                            if kc3 == 2:
                                em.touch(it, w=[avpB], r=av_slots + [Lb[f"pT{ix}"]])

                for c in range(NCH):
                    st_sims(c)
                    st_mask(c)
                    if c > 2:
                        st_tail2(c - 3)
                    st_exp(c)
                    if c > 1:
                        st_tail1(c - 2)
                st_tail1(NCH - 2)
                st_tail2(NCH - 3)
                st_tail1(NCH - 1)
                st_tail2(NCH - 2)
                st_tail2(NCH - 1)
                for bi in (6, 7):
                    it = em.o("tensor", lambda e, bi=bi: e.matmul(
                        pb[bi][:, 0:512], zero_b[0:1, 0:128], zero_b[0:1, 0:512],
                        start=False, stop=True),
                        r=[B["zero_b"]], inc=True)
                    em.touch(it, w=[pbB[bi]])
                for bi in (6, 7):
                    em.o("vector", lambda e, bi=bi, s=s: e.tensor_copy(
                        out=aout[:, (bi - 6) * 4:(bi - 6) * 4 + 4,
                                 s * 128:(s + 1) * 128],
                        in_=pb[bi][:, 0:512].rearrange("p (c t) -> p c t", c=4)),
                        r=[pbB[bi]], w=[Lb["aout"]])
            for oc in range(NCH):
                bank, pB = pb[oc % 2], pbB[oc % 2]
                g = em.grp()
                for ic in range(NCH):
                    fn = lambda e, ic=ic, oc=oc, bank=bank: e.matmul(
                        bank[:, 0:TILE], wo_sb[:, ic, oc * 128:(oc + 1) * 128],
                        aout[:, ic, :], start=(ic == 0), stop=(ic == NCH - 1))
                    if ic < NCH - 1:
                        g.mm(fn, r=[Lb["aout"], woB], first_w=(pB if ic == 0 else None))
                    else:
                        g.close(fn, r=[Lb["aout"], woB], w=[pB])
                em.o("vector", lambda e, oc=oc, bank=bank, xtile=xtile: e.tensor_tensor(
                    out=xtile[:, oc, 1:TILE + 1], in0=xtile[:, oc, 1:TILE + 1],
                    in1=bank[:, 0:TILE], op=ALU.add), r=[pB, xtB_], w=[xtB_])
            tstore(e_, ti, slot, y_dram, ydB)

    def inner_ffn_elem(l, e_, w1_sb, w1B, w2_sb, w2B, vecw1, Fb, b2row, hsb, gsb):
        tload(e_, 0, 0, y_dram, ydB, nobar=(e_ == 0))
        for ti in range(NT):
            slot = ti % 2
            xtile, xtB_ = xt[slot], xtB[slot]
            cast_xs(xtile, xtB_, halo=True)
            def ssrc(c):
                sl = xs[:, c, 1:TILE + 1] if c < 4 else xs[:, c, 0:TILE]
                return (sl, [B["xs"]])
            emit_stats(ssrc, 0, 1)
            rb_bank, rb_pB = bcast_row(rbf_row[0:1, :], [B["rbf_row"]], 0)
            cast_xs(xtile, xtB_, rb_bank, rb_pB, shifted=True)
            if ti + 1 < NT:
                tload(e_, ti + 1, (ti + 1) % 2, y_dram, ydB)
            for u in range(2):
                u0 = u * 256
                opened = [False]
                def open_acc_banks():
                    for bi in range(4, 8):
                        em.o("tensor", lambda e, bi=bi: e.matmul(
                            pb[bi][:, 0:512], zero_b[0:1, 0:128], zero_b[0:1, 0:512],
                            start=True, stop=False),
                            r=[B["zero_b"]], deps=pbB[bi].rs + [pbB[bi].w], inc=False)
                    opened[0] = True
                def st_w2(j):
                    jw = 128 if j < NJ - 1 else DI - 128 * (NJ - 1)
                    par = j % 2
                    hbuf, hB = hsb[par], Fb[f"h{par}"]
                    w2reads = []
                    for oc in range(NCH):
                        bank = pb[4 + oc // 2]
                        col = (oc % 2) * 256
                        w2reads.append(w2B[(j, oc // 4)])
                        it = em.o("tensor", lambda e, j=j, oc=oc, jw=jw, hbuf=hbuf,
                                  bank=bank, col=col: e.matmul(
                            bank[:, col:col + 256], w2_sb[0:jw, j, oc * 128:(oc + 1) * 128],
                            hbuf[0:jw, :], start=False, stop=False),
                            deps=[w2B[(j, oc // 4)].w, hB.w],
                            inc=(oc == NCH - 1))
                        if oc == NCH - 1:
                            em.touch(it, r=w2reads + [hB],
                                     w=[pbB[4], pbB[5], pbB[6], pbB[7]])

                for j in range(NJ):
                    jw = 128 if j < NJ - 1 else DI - 128 * (NJ - 1)
                    par = j % 2
                    hbuf, hB = hsb[par], Fb[f"h{par}"]
                    gbuf, gB_ = gsb[par], Fb[f"gelu{par}"]
                    gate_b, a_b = (2, 3) if par == 0 else (0, 1)
                    for half, bank_i in ((1, gate_b), (0, a_b)):
                        bank, pB = pb[bank_i], pbB[bank_i]
                        cbase = half * DI + j * 128
                        g = em.grp()
                        for kc in range(NCH):
                            fn = lambda e, kc=kc, cbase=cbase, jw=jw, bank=bank, u0=u0: \
                                e.matmul(bank[0:jw, 0:256],
                                         w1_sb[:, kc, cbase:cbase + jw],
                                         xs[:, kc, u0:u0 + 256],
                                         start=(kc == 0), stop=False)
                            g.mm(fn, r=[B["xs"], w1B[(kc, cbase // 512)]],
                                 first_w=(pB if kc == 0 else None))
                        g.close(lambda e, cbase=cbase, jw=jw, bank=bank, u0=u0:
                                e.matmul(bank[0:jw, 0:256],
                                         vecw1[0:2, cbase:cbase + jw],
                                         r1rows[0:2, u0:u0 + 256],
                                         start=False, stop=True),
                                r=[Fb["vecw1"], B["r1rows"]], w=[pB])
                    em.o("scalar", lambda e, jw=jw, gbuf=gbuf, gate_b=gate_b:
                         e.activation(
                        out=gbuf[0:jw, 0:256], in_=pb[gate_b][0:jw, 0:256],
                        func=AF.Gelu),
                        r=[pbB[gate_b]], w=[gB_])
                    em.o("vector", lambda e, jw=jw, hbuf=hbuf, gbuf=gbuf, a_b=a_b:
                         e.scalar_tensor_tensor(
                        out=hbuf[0:jw, :], in0=pb[a_b][0:jw, 0:256], scalar=1.0,
                        in1=gbuf[0:jw, 0:256], op0=ALU.mult, op1=ALU.mult),
                        r=[pbB[a_b], gB_], w=[hB])
                    if not opened[0]:
                        open_acc_banks()
                    if j > 0:
                        st_w2(j - 1)
                st_w2(NJ - 1)
                for oc in range(NCH):
                    bank = pb[4 + oc // 2]
                    bB = pbB[4 + oc // 2]
                    col = (oc % 2) * 256
                    it = em.o("tensor", lambda e, oc=oc, bank=bank, col=col,
                              last=(oc % 2 == 1): e.matmul(
                        bank[:, col:col + 256], b2row[0:1, oc * 128:(oc + 1) * 128],
                        onesrow[0:1, 0:256], start=False, stop=last),
                        r=[Fb["b2row"], B["onesrow"]], inc=True)
                    em.touch(it, w=[bB])
                for oc in range(NCH):
                    bank, bB = pb[4 + oc // 2], pbB[4 + oc // 2]
                    col = (oc % 2) * 256
                    em.o("vector", lambda e, oc=oc, bank=bank, col=col, u0=u0,
                         xtile=xtile: e.tensor_tensor(
                        out=xtile[:, oc, 1 + u0:1 + u0 + 256],
                        in0=xtile[:, oc, 1 + u0:1 + u0 + 256],
                        in1=bank[:, col:col + 256], op=ALU.add),
                        r=[bB, xtB_], w=[xtB_])
            tstore(e_, ti, slot, y_dram, ydB)

    def inner_layer(l):
        with contextlib.ExitStack() as lctx:
            lsb = lambda name, shape, dt: lctx.enter_context(
                nc.sbuf_tensor(f"{name}_l{l}", shape, dt))
            with contextlib.ExitStack() as actx:
                asb = lambda name, shape, dt: actx.enter_context(
                    nc.sbuf_tensor(f"{name}_l{l}", shape, dt))
                wqkv_sb = asb("wqkv", [128, NCH, 3 * D], BF16)
                wo_sb = asb("wo", [128, NCH, D], BF16)
                kwin = asb("kwin", [128, NCH, 768], BF16)
                vring = [asb(f"vring{i}", [128, D], BF16) for i in range(6)]
                tabs_raw = [asb(f"traw{i}", [128, 4, TILE], BF16) for i in range(2)]
                tabs_pr = asb("tpr", [128, 4, TILE], BF16)
                q_sb = asb("q", [128, NCH, 2 * TILE], BF16)
                p_sb = asb("p", [128, 8, 384], BF16)
                pT_sb = asb("pT", [128, 8, 384], BF16)
                aout = asb("aout", [128, NCH, TILE], BF16)
                rstd_b = asb("rstdb", [32, TILE], BF16)
                stage = scr_f
                Lb = {n: TBuf() for n in
                      ("wo", "tabs_pr", "aout", "rstd_b",
                       "traw0", "traw1", "q0", "q1", "rstdT0", "rstdT1")}
                kwinB = [TBuf() for _ in range(6)]
                for ix in range(8):
                    Lb[f"p{ix}"] = TBuf()
                    Lb[f"pT{ix}"] = TBuf()
                    B[f"nm{ix}"] = TBuf(); B[f"l{ix}"] = TBuf()
                    B[f"li{ix}"] = TBuf()
                vrB = [TBuf() for _ in range(6)]
                wqkvB = {(kc, g): TBuf() for kc in range(NCH) for g in range(6)}
                woB = {(kc, g): TBuf() for kc in range(NCH) for g in range(2)}
                stageB = [B["scr_f0"], B["scr_f1"]]

                em.o("sync", lambda e: e.dma_start(
                    out=qs_col[:], in_=qsks[l].rearrange("k p -> p k"),
                    allow_slow_non_contiguous=True),
                    w=[B["qs_col"]], dma="d_init")

                stream_convert(wqkv_sb, wqkvB, wqkv[l], NCH, 3 * D,
                               stage, stageB, "wqkv")
                stream_convert(wo_sb, {(kc, g): Lb["wo"] for kc in range(NCH)
                                       for g in range(2)},
                               wo[l], NCH, D, stage, stageB, "wo")

                for e_ in range(NE):
                    inner_attn_elem(l, e_, wqkv_sb, wqkvB, wo_sb, Lb, kwin,
                                    kwinB, vring, vrB, tabs_raw, tabs_pr, q_sb,
                                    p_sb, pT_sb, aout, rstd_b)
                em.phase_barrier()
                em.flush()

            with contextlib.ExitStack() as fctx:
                fsb = lambda name, shape, dt: fctx.enter_context(
                    nc.sbuf_tensor(f"{name}_l{l}", shape, dt))
                w1_sb = fsb("w1", [128, NCH, 2 * DI], BF16)
                w2_sb = fsb("w2", [128, NJ, D], BF16)
                vecw1 = fsb("vecw1", [2, 2 * DI], BF16)
                b2row = fsb("b2row", [1, D], BF16)
                glb = fsb("glb", [128, 2, NCH], BF16)
                gvec = fsb("gvec", [128, NCH], F32)
                vrow = rbf_row
                hsb = [fsb(f"h{i}", [128, 256], BF16) for i in range(2)]
                gsb = scr_sq
                stage = scr_f
                Fb = {n: TBuf() for n in ("vecw1", "b2row", "glb", "gvec",
                                          "h0", "h1")}
                Fb["vrow"] = B["rbf_row"]
                Fb["gelu0"] = B["scr_sq0"]
                Fb["gelu1"] = B["scr_sq1"]
                w1B = {(kc, g): TBuf() for kc in range(NCH)
                       for g in range((2 * DI + 511) // 512)}
                w2B = {(j, g): TBuf() for j in range(NJ) for g in range(2)}
                stageB = [B["scr_f0"], B["scr_f1"]]

                em.o("sync", lambda e: e.dma_start(
                    out=gvec[:], in_=lng[l].rearrange("(c p) -> p c", p=128),
                    allow_slow_non_contiguous=True),
                    w=[Fb["gvec"]], dma="d_init")
                em.o("gpsimd", lambda e: e.dma_start(
                    out=stage[0][:, 0:NCH],
                    in_=lnb[l].rearrange("(c p) -> p c", p=128),
                    allow_slow_non_contiguous=True),
                    w=[stageB[0]], dma="dsg0")
                em.o("vector", lambda e: e.tensor_copy(
                    out=glb[:, 1, :], in_=stage[0][:, 0:NCH]),
                    r=[stageB[0]], w=[Fb["glb"]])
                em.o("vector", lambda e: e.tensor_copy(
                    out=glb[:, 0, :], in_=gvec[:]), r=[Fb["gvec"]], w=[Fb["glb"]])

                ncols = 2 * DI
                for c0 in range(0, ncols, 512):
                    c1 = min(ncols, c0 + 512)
                    for kc in range(NCH):
                        st, stB = stage[kc % 2], stageB[kc % 2]
                        em.o("gpsimd", lambda e, kc=kc, c0=c0, c1=c1, st=st:
                             e.dma_start(out=st[:, 0:c1 - c0],
                                         in_=w1[l, kc * 128:(kc + 1) * 128, c0:c1]),
                             w=[stB], dma=f"dsg{kc % 2}")
                        sc_b = scr_sq[kc % 2]
                        scB = B[f"scr_sq{kc % 2}"]
                        em.o("vector", lambda e, c0=c0, c1=c1, st=st, sc_b=sc_b:
                             e.tensor_copy(out=sc_b[:, 0:c1 - c0], in_=st[:, 0:c1 - c0]),
                             r=[stB], w=[scB])
                        itc = em.o("tensor", lambda e, kc=kc, c0=c0, c1=c1, sc_b=sc_b:
                                   e.matmul(pb[0][0:1, 0:c1 - c0],
                                            glb[:, 0, kc:kc + 1], sc_b[:, 0:c1 - c0],
                                            start=(kc == 0), stop=(kc == NCH - 1)),
                                   r=[scB, Fb["glb"]],
                                   deps=(pbB[0].rs + [pbB[0].w] if kc == 0 else []))
                        itv = em.o("tensor", lambda e, kc=kc, c0=c0, c1=c1, sc_b=sc_b:
                                   e.matmul(pb[1][0:1, 0:c1 - c0],
                                            glb[:, 1, kc:kc + 1], sc_b[:, 0:c1 - c0],
                                            start=(kc == 0), stop=(kc == NCH - 1)),
                                   r=[scB, Fb["glb"]],
                                   deps=(pbB[1].rs + [pbB[1].w] if kc == 0 else []))
                        if kc == NCH - 1:
                            em.touch(itc, w=[pbB[0]])
                            em.touch(itv, w=[pbB[1]])
                        em.o("vector", lambda e, kc=kc, c0=c0, c1=c1, st=st:
                             e.tensor_scalar_mul(out=w1_sb[:, kc, c0:c1],
                                                 in0=st[:, 0:c1 - c0],
                                                 scalar1=gvec[:, kc:kc + 1]),
                             r=[stB, Fb["gvec"]], w=[w1B[(kc, c0 // 512)]])
                    em.o("vector", lambda e, c0=c0, c1=c1: e.tensor_copy(
                        out=vecw1[0:1, c0:c1], in_=pb[0][0:1, 0:c1 - c0]),
                        r=[pbB[0]], w=[Fb["vecw1"]])
                    st, stB = stage[0], stageB[0]
                    em.o("gpsimd", lambda e, c0=c0, c1=c1, st=st: e.dma_start(
                        out=st[0:1, 0:c1 - c0], in_=b1[l:l + 1, c0:c1]),
                        w=[stB], dma="dsg0")
                    em.o("vector", lambda e, c0=c0, c1=c1, st=st: e.tensor_tensor(
                        out=vrow[0:1, 0:c1 - c0], in0=pb[1][0:1, 0:c1 - c0],
                        in1=st[0:1, 0:c1 - c0], op=ALU.add),
                        r=[pbB[1], stB], w=[Fb["vrow"]])
                    em.o("gpsimd", lambda e, c0=c0, c1=c1: e.dma_start(
                        out=vecw1[1:2, c0:c1], in_=vrow[0:1, 0:c1 - c0]),
                        r=[Fb["vrow"]], w=[Fb["vecw1"]], dma="dvec")

                for g in range(2):
                    em.o("gpsimd", lambda e, g=g: e.dma_start(
                        out=stage[1][0:1, 0:512],
                        in_=b2[l:l + 1, g * 512:(g + 1) * 512]),
                        w=[stageB[1]], dma="dsg1")
                    em.o("vector", lambda e, g=g: e.tensor_copy(
                        out=b2row[0:1, g * 512:(g + 1) * 512],
                        in_=stage[1][0:1, 0:512]),
                        r=[stageB[1]], w=[Fb["b2row"]])

                for j in range(NJ):
                    kw = 128 if j < NJ - 1 else DI - 128 * (NJ - 1)
                    for g in range(2):
                        st, stB = stage[j % 2], stageB[j % 2]
                        em.o("gpsimd", lambda e, j=j, g=g, kw=kw, st=st: e.dma_start(
                            out=st[0:kw, 0:512],
                            in_=w2[l, j * 128:j * 128 + kw, g * 512:(g + 1) * 512]),
                            w=[stB], dma=f"dsg{j % 2}")
                        em.o("vector", lambda e, j=j, g=g, kw=kw, st=st:
                             e.tensor_copy(out=w2_sb[0:kw, j, g * 512:(g + 1) * 512],
                                           in_=st[0:kw, 0:512]),
                             r=[stB], w=[w2B[(j, g)]])

                for e_ in range(NE):
                    inner_ffn_elem(l, e_, w1_sb, w1B, w2_sb, w2B, vecw1, Fb,
                                   b2row, hsb, gsb)
                em.phase_barrier()
                em.flush()

    init_phase()
    em.o("vector", lambda e: e.memset(zero_b[:], 0.0), w=[B["zero_b"]])
    em.flush()
    outer_block()
    for l in range(NL):
        inner_layer(l)
    final_items = []
    for e_ in range(NE):
        for ti in range(NT):
            xtile, xtB_ = tload(e_, ti, ti % 2, x_dram, xdB)
            ytile2, ytB2 = tload(e_, ti, (ti + 1) % 2, y_dram, ydB)
            for c in range(NCH):
                em.o("vector", lambda e, c=c, xtile=xtile, ytile2=ytile2: e.tensor_tensor(
                    out=xtile[:, c, 1:TILE + 1], in0=xtile[:, c, 1:TILE + 1],
                    in1=ytile2[:, c, 1:TILE + 1], op=ALU.add),
                    r=[xtB_, ytB2], w=[xtB_])
            t0 = ti * TILE
            it = em.o("sync", lambda e, t0=t0, e_=e_, xtile=xtile: e.dma_start(
                out=yout[e_, :, t0:t0 + TILE].rearrange("(c p) t -> p c t", p=128),
                in_=xtile[:, :, 1:TILE + 1]), r=[xtB_], dma="d_out")
            final_items.append(it)
    em.o("gpsimd", lambda e: e.nop(), deps=[final_items[-1]])
    em.flush()
    ctx.close()
    return nc



def _host_constants(T):
    tabs = _rope_tables(T).astype(BF_NP)
    masks = _masks().astype(BF_NP)
    identf = np.eye(128, dtype=np.float32)
    bias32 = np.zeros((32, 2), dtype=np.float32)
    bias32[0:16, 0] = np.log(8.0)
    bias32[0:16, 1] = 8e12
    bias32[16:32, 1] = 1e12
    selss = _sel_ss().astype(BF_NP)
    sel2k = _sel2k().astype(BF_NP)
    selst = _sel_stat().astype(BF_NP)
    return dict(tabs=tabs, masks=masks, identf=identf, bias32=bias32,
                selss=selss, sel2k=sel2k, selst=selst)


def kernel(**inputs):
    T, NL, NE, NCORE = 4096, 6, 2, 8
    x = np.asarray(inputs["L0_attn_x"])
    colp = _colperm_qkv()
    wqkv = np.ascontiguousarray(np.asarray(inputs["L0_ffn_Wqkv"])[:, :, colp])
    qs = np.asarray(inputs["L0_ffn_q_scale"])
    ks = np.asarray(inputs["L0_ffn_k_scale"])
    qsks = np.zeros((NL, 4, 128), dtype=np.float32)
    for l in range(NL):
        qp = qs[l][SIG]; kp = ks[l][SIG]
        qsks[l, 0, :] = np.concatenate([qp, qp])
        qsks[l, 2, :] = np.concatenate([kp, kp])
        qsks[l, 1, :] = qsks[l, 0, SHUF_FULL]
        qsks[l, 3, :] = qsks[l, 2, SHUF_FULL]
    consts = _host_constants(T)
    common = dict(
        wq=np.ascontiguousarray(inputs["L0_attn_Wq"]),
        wk=np.ascontiguousarray(inputs["L0_attn_Wk"]),
        wv=np.ascontiguousarray(inputs["L0_attn_Wv"]),
        bqv=np.ascontiguousarray(np.stack([inputs["L0_attn_bq"],
                                           inputs["L0_attn_bk"],
                                           inputs["L0_attn_bv"]])),
        wqkv=wqkv,
        wo=np.ascontiguousarray(inputs["L0_ffn_Wo"]),
        w1=np.ascontiguousarray(inputs["L0_ffn_W1"]),
        b1=np.ascontiguousarray(inputs["L0_ffn_b1"]),
        w2=np.ascontiguousarray(inputs["L0_ffn_W2"]),
        b2=np.ascontiguousarray(inputs["L0_ffn_b2"]),
        lng=np.ascontiguousarray(inputs["L0_ffn_ln_g"]),
        lnb=np.ascontiguousarray(inputs["L0_ffn_ln_b"]),
        qsks=qsks, **consts)
    nc = build_program(T=T, NL=NL, NE=NE)
    in_maps = []
    for ci in range(NCORE):
        xs_ = np.ascontiguousarray(
            x[ci * NE:(ci + 1) * NE].transpose(0, 2, 1))
        m = dict(common)
        m["x0"] = xs_
        in_maps.append(m)
    res = run_bass_kernel_spmd(nc, in_maps, core_ids=list(range(NCORE)))
    out = np.empty((16, T, D), dtype=np.float32)
    for ci in range(NCORE):
        y = res.results[ci]["yout"]
        out[ci * NE:(ci + 1) * NE] = y.transpose(0, 2, 1)
    return out


# version 16
# speedup vs baseline: 1.0121x; 1.0121x over previous
import numpy as np
import ml_dtypes

import concourse.bass as bass
import concourse.mybir as mybir
from concourse.bass_utils import run_bass_kernel_spmd

F32 = mybir.dt.float32
BF16 = mybir.dt.bfloat16
AF = mybir.ActivationFunctionType
ALU = mybir.AluOpType
AX = mybir.AxisListType

D = 1024
NCH = 8
HEADS = 16
DH = 64
WIN = 256
DI = 2730
NJ = 22
TILE = 512
BF_NP = ml_dtypes.bfloat16


def _sigma_perm():
    s = np.zeros(64, dtype=np.int64)
    for j in range(16):
        s[2 * j] = j;         s[2 * j + 1] = 32 + j
        s[32 + 2 * j] = 16 + j; s[32 + 2 * j + 1] = 48 + j
    return s

SIG = _sigma_perm()
SHUF32 = sum([[2 * j + 1, 2 * j] for j in range(16)], [])
SHUF_FULL = np.concatenate([32 * q + np.array(SHUF32) for q in range(4)])

def _colperm_qkv():
    perm = np.arange(3 * D)
    for base in (0, D):
        for h in range(HEADS):
            cols = base + h * DH + np.arange(DH)
            perm[base + h * DH: base + h * DH + DH] = cols[SIG]
    return perm

def _rope_tables(T):
    half = np.arange(0, DH, 2, dtype=np.float64)
    inv_freq = 1.0 / (10000.0 ** (half / DH))
    t = np.arange(T, dtype=np.float64)
    freqs = np.concatenate([t[:, None] * inv_freq[None, :]] * 2, axis=1)
    base_scale = (half + 0.4 * DH) / (1.4 * DH)
    power = (t - T // 2) / (WIN // 2)
    scale = np.concatenate([base_scale[None, :] ** power[:, None]] * 2, axis=1)
    cos, sin = np.cos(freqs), np.sin(freqs)
    sgn = np.where(np.arange(DH) < 32, -1.0, 1.0)
    lay = lambda v: v.T[SIG]
    Aq = lay(cos * scale); Bq = lay(sin * scale) * sgn[SIG][:, None]
    Ak = lay(cos / scale); Bk = lay(sin / scale) * sgn[SIG][:, None]
    dup = lambda m: np.concatenate([m, m], axis=0).astype(np.float32)
    return np.stack([dup(Aq), dup(Bq), dup(Ak), dup(Bk)])

def _masks():
    i = np.arange(128)[:, None]
    c = np.arange(384)[None, :]
    band = (i <= c) & (c <= i + WIN)
    m = np.where(band, 0.0, -1e30)
    m1 = np.where(band & (c >= 128), 0.0, -1e30)
    m0 = np.where(band & (c >= 256), 0.0, -1e30)
    return np.stack([m, m1, m0]).astype(np.float32)

def _sel_ss():
    sel = np.zeros((16, 128, 32), dtype=np.float32)
    for m in range(16):
        sel[m, 0:64, 2 * m] = 1.0
        sel[m, 64:128, 2 * m + 1] = 1.0
    return sel

def _sel2k():
    sel = np.zeros((8, 32, 128), dtype=np.float32)
    for c in range(8):
        sel[c, 16 + 2 * c, 0:64] = 1.0
        sel[c, 16 + 2 * c + 1, 64:128] = 1.0
    return sel

def _sel_stat():
    sel = np.zeros((2, 128, 2), dtype=np.float32)
    sel[0, :, 0] = 1.0
    sel[1, :, 1] = 1.0
    return sel



ENGS = ("tensor", "vector", "scalar", "gpsimd", "sync")

class It:
    __slots__ = ("sem", "cnt")
    def __init__(self, sem, cnt):
        self.sem = sem; self.cnt = cnt

class TBuf:
    __slots__ = ("w", "rs")
    def __init__(self):
        self.w = None; self.rs = []

class _MMGroup:
    def __init__(self, em):
        self.em = em; self.r = []
    def mm(self, fn, r=(), deps=(), first_w=None):
        dd = list(deps)
        for b in r:
            dd.append(b.w)
        if first_w is not None:
            dd.extend(first_w.rs); dd.append(first_w.w)
        self.em.op("tensor", fn, dd, inc=False)
        self.r.extend(r)
    def close(self, fn, r=(), w=(), deps=()):
        dd = list(deps)
        for b in r:
            dd.append(b.w)
        for b in w:
            dd.extend(b.rs); dd.append(b.w)
        it = self.em.op("tensor", fn, dd, inc=True)
        self.em.touch(it, r=self.r + list(r), w=w)
        return it


class Em:
    def __init__(self, nc, sems):
        self.nc = nc
        self.sems = sems
        self.cnt = {k: 0 for k in sems}
        self.ops = {e: [] for e in ENGS}
        self.marks = {e: {k: 0 for k in sems} for e in ENGS}
        self.nblk = 0
        self.pending_barrier = None

    def _waits(self, eng, deps):
        need = {}
        for d in deps:
            if d is None:
                continue
            if isinstance(d, (list, tuple)):
                for dd in d:
                    if dd is not None and dd.cnt > need.get(dd.sem, 0):
                        need[dd.sem] = dd.cnt
            elif d.cnt > need.get(d.sem, 0):
                need[d.sem] = d.cnt
        w = []
        for s, n in sorted(need.items()):
            if not s.startswith("c_"):
                if s in ("d_init", "d_out", "dvec"):
                    n = self.cnt[s]
                else:
                    assert n == self.cnt[s], (
                        f"unsound DMA wait on {s}: want {n}, issued {self.cnt[s]}")
            if self.marks[eng][s] >= n:
                continue
            w.append((s, n))
            self.marks[eng][s] = n
        return w

    def op(self, eng, fn, deps=(), inc=True, dma=None, nobar=False):
        sem = dma if dma else ("c_" + eng)
        if self.pending_barrier and not nobar:
            deps = list(deps) + self.pending_barrier
        waits = self._waits(eng, deps)
        if inc:
            amt = 16 if dma else 1
            self.cnt[sem] += amt
            item = It(sem, self.cnt[sem])
            self.ops[eng].append((fn, waits, (sem, amt)))
            return item
        self.ops[eng].append((fn, waits, None))
        return None

    def o(self, eng, fn, r=(), w=(), deps=(), inc=True, dma=None, nobar=False):
        dd = list(deps)
        for b in r:
            dd.append(b.w)
        for b in w:
            dd.extend(b.rs); dd.append(b.w)
        it = self.op(eng, fn, dd, inc=inc, dma=dma, nobar=nobar)
        if it is not None:
            for b in r:
                b.rs.append(it)
            for b in w:
                b.w = it; b.rs = []
        return it

    def grp(self):
        return _MMGroup(self)

    def phase_barrier(self):
        dma_items = [It(s, n) for s, n in self.cnt.items()
                     if n > 0 and not s.startswith("c_")]
        fence = self.op("sync", lambda e: e.nop(), deps=dma_items)
        self.pending_barrier = [It(s, n) for s, n in self.cnt.items()
                                if n > 0 and s.startswith("c_")] + [fence]

    def touch(self, it, r=(), w=()):
        for b in r:
            b.rs.append(it)
        for b in w:
            b.w = it; b.rs = []

    def flush(self):
        if not any(self.ops.values()):
            return
        self.nblk += 1
        with self.nc.Block(f"blk{self.nblk}") as block:
            for e in ENGS:
                lst = self.ops[e]
                if not lst:
                    continue
                def emit(eng, lst=lst):
                    for fn, waits, inc in lst:
                        for s, n in waits:
                            eng.wait_ge(self.sems[s], n)
                        ins = fn(eng)
                        if inc is not None:
                            ins.then_inc(self.sems[inc[0]], inc[1])
                getattr(block, e)(emit)
        self.ops = {e: [] for e in ENGS}



def build_program(T=4096, NL=6, NE=2):
    NT = T // TILE
    nc = bass.Bass("TRN2", target_bir_lowering=False)

    ap = lambda h: h.ap()
    x0 = ap(nc.dram_tensor("x0", [NE, D, T], F32, kind="ExternalInput"))
    wq = ap(nc.dram_tensor("wq", [D, D], F32, kind="ExternalInput"))
    wk = ap(nc.dram_tensor("wk", [D, D], F32, kind="ExternalInput"))
    wv = ap(nc.dram_tensor("wv", [D, D], F32, kind="ExternalInput"))
    bqv = ap(nc.dram_tensor("bqv", [3, D], F32, kind="ExternalInput"))
    wqkv = ap(nc.dram_tensor("wqkv", [NL, D, 3 * D], F32, kind="ExternalInput"))
    wo = ap(nc.dram_tensor("wo", [NL, D, D], F32, kind="ExternalInput"))
    w1 = ap(nc.dram_tensor("w1", [NL, D, 2 * DI], F32, kind="ExternalInput"))
    b1 = ap(nc.dram_tensor("b1", [NL, 2 * DI], F32, kind="ExternalInput"))
    w2 = ap(nc.dram_tensor("w2", [NL, DI, D], F32, kind="ExternalInput"))
    b2 = ap(nc.dram_tensor("b2", [NL, D], F32, kind="ExternalInput"))
    lng = ap(nc.dram_tensor("lng", [NL, D], F32, kind="ExternalInput"))
    lnb = ap(nc.dram_tensor("lnb", [NL, D], F32, kind="ExternalInput"))
    qsks = ap(nc.dram_tensor("qsks", [NL, 4, 128], F32, kind="ExternalInput"))
    tabs_in = ap(nc.dram_tensor("tabs", [4, 128, T], BF16, kind="ExternalInput"))
    identf_in = ap(nc.dram_tensor("identf", [128, 128], F32, kind="ExternalInput"))
    masks_in = ap(nc.dram_tensor("masks", [3, 128, 384], BF16, kind="ExternalInput"))
    bias32_in = ap(nc.dram_tensor("bias32", [32, 2], F32, kind="ExternalInput"))
    selss_in = ap(nc.dram_tensor("selss", [16, 128, 32], BF16, kind="ExternalInput"))
    sel2k_in = ap(nc.dram_tensor("sel2k", [8, 32, 128], BF16, kind="ExternalInput"))
    selst_in = ap(nc.dram_tensor("selst", [2, 128, 2], BF16, kind="ExternalInput"))

    yout = ap(nc.dram_tensor("yout", [NE, D, T], F32, kind="ExternalOutput"))

    x_dram = ap(nc.dram_tensor("x_scr", [NE, D, T + 1], F32, kind="Internal"))
    y_dram = ap(nc.dram_tensor("y_scr", [NE, D, T + 1], F32, kind="Internal"))

    xdB = [[TBuf() for _ in range(NT)] for _ in range(NE)]
    ydB = [[TBuf() for _ in range(NT)] for _ in range(NE)]

    import contextlib
    ctx = contextlib.ExitStack()

    sems = {}
    semnames = ["c_tensor", "c_vector", "c_scalar", "c_gpsimd", "c_sync",
                "d_init", "d_out", "dxa", "dxb", "dsg0", "dsg1", "dsg2", "dsg3",
                "dtab0", "dtab1", "dvec"]
    for st_ in ("x", "y"):
        for e_ in range(NE):
            semnames += [f"dst_{st_}{e_}_{i}" for i in range(NT)]
    for s in semnames:
        sems[s] = ctx.enter_context(nc.semaphore(s))
    em = Em(nc, sems)

    sb = lambda name, shape, dt: ctx.enter_context(nc.sbuf_tensor(name, shape, dt))
    ps = lambda name, shape, dt=F32: ctx.enter_context(nc.psum_tensor(name, shape, dt))

    xt = [sb(f"xt{i}", [128, NCH, TILE + 1], F32) for i in range(2)]
    xs = sb("xs", [128, NCH, TILE + 1], BF16)
    identf = sb("s_identf", [128, 128], F32)
    identb = sb("s_identb", [128, 128], BF16)
    masks = sb("s_masks", [128, 3, 384], BF16)
    bias32 = sb("s_bias32", [32, 2], F32)
    selss = sb("s_selss", [128, 16, 32], BF16)
    sel2k = sb("s_sel2k", [32, 8, 128], BF16)
    selst = sb("s_selst", [128, 2, 2], BF16)
    onesrow = sb("onesrow", [1, 512], BF16)
    onescol = sb("onescol", [128, 1], BF16)
    r1rows = sb("r1rows", [2, TILE], BF16)
    strows = sb("strows", [1, TILE * 2], F32)
    srow = lambda i: strows[:, [0, 0, 1, 0, 0][i] * TILE:([0, 0, 1, 0, 0][i] + 1) * TILE]
    rbf_row = sb("rbf_row", [1, TILE], BF16)
    rstd_sb = sb("rstd_sb", [32, TILE], F32)
    rstdT = sb("rstdT", [128, 256], F32)
    scr_sq = [sb(f"scr_sq{i}", [128, TILE], BF16) for i in range(2)]
    scr_f = [sb(f"scr_f{i}", [128, TILE], F32) for i in range(2)]
    l_sb = sb("l_sb", [128, 8], F32)
    linv_sb = sb("linv_sb", [128, 8], F32)
    nm_sb = sb("nm_sb", [128, 8], F32)
    qs_col = sb("qs_col", [128, 4], F32)
    cvals = sb("cvals", [128, 3], F32)
    halo_sb = sb("halo_sb", [128, NCH], F32)

    B = {}
    for name, t in [("xt0", xt[0]), ("xt1", xt[1]), ("xs", xs), ("identf", identf),
                    ("masks", masks), ("bias32", bias32), ("selss", selss),
                    ("sel2k", sel2k), ("selst", selst), ("onesrow", onesrow),
                    ("onescol", onescol), ("r1rows", r1rows),
                    ("strows", strows), ("rbf_row", rbf_row), ("rstd_sb", rstd_sb),
                    ("rstdT", rstdT), ("scr_sq0", scr_sq[0]), ("scr_sq1", scr_sq[1]),
                    ("scr_f0", scr_f[0]), ("scr_f1", scr_f[1]),
                    ("l_sb", l_sb), ("linv_sb", linv_sb), ("nm_sb", nm_sb),
                    ("qs_col", qs_col), ("cvals", cvals), ("halo_sb", halo_sb),
                    ("identb", identb)]:
        B[name] = TBuf()
    xtB = [B["xt0"], B["xt1"]]

    pb = [ps(f"pb{i}", [128, 512], F32) for i in range(8)]
    pbB = [TBuf() for _ in range(8)]

    zero_b = sb("zero_b", [128, TILE], BF16)
    B["zero_b"] = TBuf()

    def init_phase():
        ld = []
        ld.append(em.o("sync", lambda e: e.dma_start(out=identf[:], in_=identf_in),
                       w=[B["identf"]], dma="d_init"))
        em.o("sync", lambda e: e.dma_start(
            out=masks[:], in_=masks_in.rearrange("m p k -> p m k")),
            w=[B["masks"]], dma="d_init")
        em.o("sync", lambda e: e.dma_start(out=bias32[:], in_=bias32_in),
             w=[B["bias32"]], dma="d_init")
        em.o("sync", lambda e: e.dma_start(out=selss[:], in_=selss_in.rearrange("m p k -> p m k")),
             w=[B["selss"]], dma="d_init")
        em.o("sync", lambda e: e.dma_start(out=sel2k[:], in_=sel2k_in.rearrange("c p k -> p c k")),
             w=[B["sel2k"]], dma="d_init")
        em.o("sync", lambda e: e.dma_start(out=selst[:], in_=selst_in.rearrange("m p k -> p m k")),
             w=[B["selst"]], dma="d_init")
        em.o("vector", lambda e: e.memset(onesrow[:], 1.0), w=[B["onesrow"]])
        em.o("vector", lambda e: e.tensor_copy(out=identb[:], in_=identf[:]),
             r=[B["identf"]], w=[B["identb"]])
        em.o("vector", lambda e: e.memset(cvals[:, 0:1], 1e-5), w=[B["cvals"]])
        em.o("vector", lambda e: e.memset(cvals[:, 1:2], 1e-6 / T), w=[B["cvals"]])
        em.o("vector", lambda e: e.memset(cvals[:, 2:3], 1e-30), w=[B["cvals"]])
        em.o("vector", lambda e: e.memset(onescol[:], 1.0), w=[B["onescol"]])
        em.o("vector", lambda e: e.memset(r1rows[:], 1.0), w=[B["r1rows"]])
        em.o("vector", lambda e: e.memset(scr_f[0][:, 0:NCH], 0.0), w=[B["scr_f0"]])
        for e_ in range(NE):
            for dram, dbuf in ((x_dram, xdB), (y_dram, ydB)):
                em.o("sync", lambda eng, e_=e_, dram=dram: eng.dma_start(
                    out=dram[e_, :, 0:1].rearrange("(c p) t -> p c t", p=128),
                    in_=scr_f[0][:, 0:NCH].rearrange("p (c t) -> p c t", t=1),
                    allow_slow_non_contiguous=True),
                    r=[B["scr_f0"]], w=[dbuf[e_][0]], dma="d_init")

    def emit_stats(src_get, b1, b2):
        bankA, pA = pb[b1], pbB[b1]
        bankB, pB_ = pb[b2], pbB[b2]
        for c in range(NCH):
            src, rb = src_get(c)
            sq, sqB = scr_sq[c % 2], B[f"scr_sq{c % 2}"]
            em.o("scalar", lambda e, src=src, sq=sq: e.activation(
                out=sq[:, 0:TILE], in_=src, func=AF.Square), r=rb, w=[sqB])
            itA = em.o("tensor", lambda e, src=src, c=c: e.matmul(
                bankA[0:1, 0:TILE], onescol[:, 0:1], src,
                start=(c == 0), stop=(c == NCH - 1)),
                r=rb + [B["onescol"]],
                deps=(pA.rs + [pA.w] if c == 0 else []))
            itB = em.o("tensor", lambda e, sq=sq, c=c: e.matmul(
                bankB[0:1, 0:TILE], onescol[:, 0:1], sq[:, 0:TILE],
                start=(c == 0), stop=(c == NCH - 1)),
                r=[sqB, B["onescol"]],
                deps=(pB_.rs + [pB_.w] if c == 0 else []))
            if c == NCH - 1:
                em.touch(itA, w=[pA])
                em.touch(itB, w=[pB_])
        em.o("scalar", lambda e: e.activation(out=srow(1), in_=bankA[0:1, 0:TILE],
                                              func=AF.Square, scale=1.0 / D),
             r=[pA], w=[B["strows"]])
        em.o("vector", lambda e: e.scalar_tensor_tensor(
            out=srow(1), in0=bankB[0:1, 0:TILE], scalar=1.0 / D, in1=srow(1),
            op0=ALU.mult, op1=ALU.subtract), r=[pB_, B["strows"]], w=[B["strows"]])
        em.o("scalar", lambda e: e.activation(out=srow(1), in_=srow(1), func=AF.Ln,
                                              bias=cvals[0:1, 0:1]),
             r=[B["strows"], B["cvals"]], w=[B["strows"]])
        em.o("scalar", lambda e: e.activation(out=srow(2), in_=srow(1), func=AF.Exp,
                                              scale=-0.5), r=[B["strows"]], w=[B["strows"]])
        em.o("vector", lambda e: e.scalar_tensor_tensor(
            out=srow(1), in0=bankA[0:1, 0:TILE], scalar=1.0 / D, in1=srow(2),
            op0=ALU.mult, op1=ALU.mult), r=[pA, B["strows"]], w=[B["strows"]])
        em.o("vector", lambda e: e.tensor_scalar_mul(r1rows[0:1, :], srow(1), -1.0),
             r=[B["strows"]], w=[B["r1rows"]])
        em.o("vector", lambda e: e.tensor_copy(out=rbf_row[:], in_=srow(2)),
             r=[B["strows"]], w=[B["rbf_row"]])

    def bcast_row(src_ap, src_bufs, bank_i):
        bank, pB = pb[bank_i], pbB[bank_i]
        it = em.o("tensor", lambda e: e.matmul(bank[:, 0:TILE], onesrow[0:1, 0:128],
                                               src_ap, start=True, stop=True),
                  r=[B["onesrow"]] + list(src_bufs), w=[pB])
        return bank, pB

    def stream_convert(dst, dstB, src_ap, kparts, cols, stage, stageB, tag,
                       scale=None, pp_scalar=None, post=None, dma_eng="gpsimd"):
        ns = len(stage)
        pi = 0
        for kc in range(kparts):
            for c0 in range(0, cols, 512):
                c1 = min(cols, c0 + 512)
                st = stage[pi % ns]
                stB = stageB[pi % ns]
                em.o(dma_eng, lambda e, kc=kc, c0=c0, c1=c1, st=st: e.dma_start(
                    out=st[:, 0:c1 - c0], in_=src_ap[kc * 128:(kc + 1) * 128, c0:c1]),
                    w=[stB], dma=f"dsg{pi % ns}")
                pi += 1
                if pp_scalar is None:
                    itc = em.o("vector", lambda e, kc=kc, c0=c0, c1=c1, st=st: e.tensor_copy(
                        out=dst[:, kc, c0:c1], in_=st[:, 0:c1 - c0]),
                        r=[stB], w=[dstB[(kc, c0 // 512)]])
                else:
                    sc_ap, sc_bufs = pp_scalar(kc)
                    em.o("vector", lambda e, kc=kc, c0=c0, c1=c1, st=st, sc_ap=sc_ap:
                         e.tensor_scalar_mul(out=dst[:, kc, c0:c1], in0=st[:, 0:c1 - c0],
                                             scalar1=sc_ap),
                         r=[stB] + list(sc_bufs), w=[dstB[(kc, c0 // 512)]])
                if post is not None:
                    post(kc, c0, c1)

    def outer_block():
        with contextlib.ExitStack() as octx:
            osb = lambda name, shape, dt: octx.enter_context(nc.sbuf_tensor(name, shape, dt))
            wk_sb = osb("wk_sb", [128, NCH, D], BF16)
            wv_sb = osb("wv_sb", [128, NCH, D], BF16)
            wq_sb = osb("wq_sb", [128, NCH, D], BF16)
            vec_k = osb("vec_k", [2, D], BF16)
            vec_v = osb("vec_v", [2, D], BF16)
            vec_q = osb("vec_q", [2, D], BF16)
            kv_sb = osb("kv_sb", [128, NCH, 128], BF16)
            kst_sb = osb("kst_sb", [128, NCH], BF16)
            ktok = osb("ktok", [128, D], BF16)
            vtok = osb("vtok", [128, D], BF16)
            q_sb = osb("q_sb", [128, NCH, TILE], BF16)
            yt = osb("yt", [128, NCH, TILE], F32)
            zrow = osb("zrow", [1, 2 * TILE], F32)
            zrow_b = osb("zrow_b", [1, 2 * TILE], BF16)
            oelu = osb("oelu", [128, 2, TILE], F32)
            B["oelu0"] = TBuf(); B["oelu1"] = TBuf()
            stage = [osb(f"ostage{i}", [128, 512], F32) for i in range(2)]
            vecrow = osb("vecrow", [1, 1024], BF16)
            B["vecrow"] = TBuf()
            for nm in ("wk_sb", "wv_sb", "wq_sb", "vec_k", "vec_v", "vec_q",
                       "kst_sb", "ktok", "vtok", "q_sb", "yt", "zrow", "zrow_b"):
                B[nm] = TBuf()
            kvB = {}
            for h in range(NCH):
                kvB[h] = TBuf()
            stageB = [TBuf(), TBuf()]
            wkB = {}; wvB = {}; wqB = {}
            for kc in range(NCH):
                for g in range(2):
                    wkB[(kc, g)] = TBuf(); wvB[(kc, g)] = TBuf(); wqB[(kc, g)] = TBuf()

            def wconvert(dst, dstB, src, vec, vecB, bias_row, scale):
                stream_convert(dst, dstB, src, NCH, D, stage, stageB, "w",
                               pp_scalar=None if scale is None else None)
                if scale is not None:
                    pass
                for half in range(2):
                    g = em.grp()
                    for kc in range(NCH):
                        fn = lambda e, kc=kc, half=half: e.matmul(
                            pb[0][0:1, 0:512], onescol[:, 0:1],
                            dst[:, kc, half * 512:(half + 1) * 512],
                            start=(kc == 0), stop=(kc == NCH - 1))
                        if kc < NCH - 1:
                            g.mm(fn, r=[dstB[(kc, half)], B["onescol"]],
                                 first_w=(pbB[0] if kc == 0 else None))
                        else:
                            g.close(fn, r=[dstB[(kc, half)], B["onescol"]], w=[pbB[0]])
                    em.o("vector", lambda e, half=half: e.tensor_copy(
                        out=vec[0:1, half * 512:(half + 1) * 512],
                        in_=pb[0][0:1, 0:512]), r=[pbB[0]], w=[vecB])
                em.o("gpsimd", lambda e: e.dma_start(out=stage[0][0:1, 0:512],
                                                     in_=bias_row[0:1, 0:512]),
                     w=[stageB[0]], dma="dsg0")
                em.o("gpsimd", lambda e: e.dma_start(out=stage[1][0:1, 0:512],
                                                     in_=bias_row[0:1, 512:1024]),
                     w=[stageB[1]], dma="dsg1")
                sc = 1.0 if scale is None else scale
                em.o("vector", lambda e, sc=sc: e.tensor_scalar_mul(
                    out=vecrow[0:1, 0:512], in0=stage[0][0:1, 0:512], scalar1=sc),
                    r=[stageB[0]], w=[B["vecrow"]])
                em.o("vector", lambda e, sc=sc: e.tensor_scalar_mul(
                    out=vecrow[0:1, 512:1024], in0=stage[1][0:1, 0:512], scalar1=sc),
                    r=[stageB[1]], w=[B["vecrow"]])
                em.o("gpsimd", lambda e: e.dma_start(out=vec[1:2, 0:1024],
                                                     in_=vecrow[0:1, 0:1024]),
                     r=[B["vecrow"]], w=[vecB], dma="dvec")

            wconvert(wk_sb, wkB, wk, vec_k, B["vec_k"], bqv[1:2, :], None)
            wconvert(wv_sb, wvB, wv, vec_v, B["vec_v"], bqv[2:3, :], 1.0 / T)
            wconvert(wq_sb, wqB, wq, vec_q, B["vec_q"], bqv[0:1, :], None)
            for kc in range(NCH):
                for g in range(2):
                    em.o("vector", lambda e, kc=kc, g=g: e.tensor_scalar_mul(
                        out=wv_sb[:, kc, g * 512:(g + 1) * 512],
                        in0=wv_sb[:, kc, g * 512:(g + 1) * 512], scalar1=1.0 / T),
                        r=[wvB[(kc, g)]], w=[wvB[(kc, g)]])
            em.o("vector", lambda e: e.tensor_scalar_mul(out=vec_v[0:1, :],
                                                         in0=vec_v[0:1, :], scalar1=1.0 / T),
                 r=[B["vec_v"]], w=[B["vec_v"]])

            for e_ in range(NE):
                outer_pass1(e_, wk_sb, wkB, wv_sb, wvB, vec_k, vec_v,
                            kv_sb, kvB, kst_sb, ktok, vtok, stage, stageB, oelu)
                outer_pass2(e_, wq_sb, wqB, vec_q, kv_sb, kvB, kst_sb,
                            q_sb, yt, zrow, zrow_b, oelu)
            em.phase_barrier()
            em.flush()

    def tload(e_, ti, slot, dram, dramB, halo=True, nobar=False):
        t0 = ti * TILE
        dst = xt[slot]
        tB = xtB[slot]
        src = dram[e_, :, t0 + 1:t0 + TILE + 1].rearrange("(c p) t -> p c t", p=128)
        dep = dramB[e_][ti]
        it = em.o("sync", lambda e: e.dma_start(out=dst[:, :, 1:TILE + 1], in_=src),
                  w=[tB], deps=[dep.w], dma=("dxa" if slot == 0 else "dxb"),
                  nobar=nobar)
        dep.rs.append(it)
        if halo:
            if ti == 0:
                em.o("vector", lambda e: e.memset(halo_sb[:], 0.0),
                     w=[B["halo_sb"]], nobar=nobar)
            em.o("vector", lambda e: e.tensor_copy(
                out=dst[:, :, 0:1], in_=halo_sb[:].rearrange("p (c o) -> p c o", o=1)),
                r=[B["halo_sb"]], w=[tB], nobar=nobar)
            em.o("vector", lambda e: e.tensor_copy(
                out=halo_sb[:].rearrange("p (c o) -> p c o", o=1),
                in_=dst[:, :, TILE:TILE + 1]), r=[tB], w=[B["halo_sb"]], nobar=nobar)
        else:
            em.o("vector", lambda e: e.memset(dst[:, :, 0:1], 0.0), w=[tB],
                 nobar=nobar)
        return dst, tB

    def tstore(e_, ti, slot, dram, dramB, src=None, srcB=None):
        t0 = ti * TILE
        s = xt[slot] if src is None else src
        sB = xtB[slot] if srcB is None else srcB
        dstap = dram[e_, :, t0 + 1:t0 + TILE + 1].rearrange("(c p) t -> p c t", p=128)
        sl = s[:, :, 1:TILE + 1] if src is None else s[:, :, 0:TILE]
        sname = "x" if dram is x_dram else "y"
        it = em.o("sync", lambda e: e.dma_start(out=dstap, in_=sl),
                  r=[sB], w=[dramB[e_][ti]], dma=f"dst_{sname}{e_}_{ti}")
        return it

    def cast_xs(xtile, xtB_, scaled_bank=None, scaled_pB=None, shifted=False,
                halo=False):
        its = []
        if scaled_bank is None:
            sl = slice(0, TILE + 1) if halo else slice(1, TILE + 1)
            for c in range(NCH):
                its.append(em.o("vector", lambda e, c=c, sl=sl: e.tensor_copy(
                    out=xs[:, c, sl], in_=xtile[:, c, sl]), r=[xtB_], w=[B["xs"]]))
        else:
            for c in range(NCH):
                sl = slice(1, TILE + 1) if (not shifted or c < 4) else slice(0, TILE)
                its.append(em.o("vector", lambda e, c=c, sl=sl: e.tensor_tensor(
                    out=xs[:, c, 0:TILE], in0=xtile[:, c, sl],
                    in1=scaled_bank[:, 0:TILE], op=ALU.mult),
                    r=[xtB_, scaled_pB], w=[B["xs"]]))
        return its

    def outer_pass1(e_, wk_sb, wkB, wv_sb, wvB, vec_k, vec_v, kv_sb, kvB,
                    kst_sb, ktok, vtok, stage, stageB, oelu):
        def oload(ti):
            slot = ti % 2
            xtile, xtB_ = xt[slot], xtB[slot]
            t0 = ti * TILE
            srcap = x0[e_, :, t0:t0 + TILE].rearrange("(c p) t -> p c t", p=128)
            em.o("sync", lambda e, srcap=srcap, xtile=xtile: e.dma_start(
                out=xtile[:, :, 1:TILE + 1], in_=srcap), w=[xtB_],
                dma=("dxa" if slot == 0 else "dxb"))

        oload(0)
        for ti in range(T // TILE):
            slot = ti % 2
            xtile, xtB_ = xt[slot], xtB[slot]
            t0 = ti * TILE
            if ti + 1 < T // TILE:
                oload(ti + 1)
            cast_xs(xtile, xtB_)
            emit_stats(lambda c: (xs[:, c, 1:TILE + 1], [B["xs"]]), 2, 3)
            em.o("vector", lambda e: e.tensor_copy(out=rbf_row[:], in_=srow(2)),
                 r=[B["strows"]], w=[B["rbf_row"]])
            rb_bank, rb_pB = bcast_row(rbf_row[0:1, :], [B["rbf_row"]], 2)
            cast_xs(xtile, xtB_, rb_bank, rb_pB)
            for ts in range(TILE // 128):
                tc = ti * 4 + ts
                for (wsb, wB, vec, vecB, dstT, dstB_) in (
                        (wk_sb, wkB, vec_k, B["vec_k"], ktok, B["ktok"]),
                        (wv_sb, wvB, vec_v, B["vec_v"], vtok, B["vtok"])):
                    for half in range(2):
                        bank, pB = pb[ts % 2], pbB[ts % 2]
                        g = em.grp()
                        for kc in range(NCH):
                            g.mm(lambda e, kc=kc, half=half, ts=ts, wsb=wsb, bank=bank:
                                 e.matmul(bank[:, 0:512],
                                          xs[:, kc, ts * 128:ts * 128 + 128],
                                          wsb[:, kc, half * 512:(half + 1) * 512],
                                          start=(kc == 0), stop=False),
                                 r=[B["xs"], wB[(kc, half)]],
                                 first_w=(pB if kc == 0 else None))
                        g.close(lambda e, half=half, vec=vec, bank=bank, ts=ts:
                                e.matmul(bank[:, 0:512],
                                         r1rows[0:2, ts * 128:ts * 128 + 128],
                                         vec[0:2, half * 512:(half + 1) * 512],
                                         start=False, stop=True),
                                r=[B["r1rows"], vecB], w=[pB])
                        if dstT is ktok:
                            ep = (ts * 2 + half) % 2
                            epB = B[f"oelu{ep}"]
                            em.o("vector", lambda e, bank=bank, ep=ep: e.tensor_scalar_min(
                                out=oelu[:, ep, :], in0=bank[:, 0:512], scalar1=0.0),
                                r=[pB], w=[epB])
                            em.o("scalar", lambda e, ep=ep: e.activation(
                                out=oelu[:, ep, :], in_=oelu[:, ep, :], func=AF.Exp),
                                r=[epB], w=[epB])
                            em.o("vector", lambda e, bank=bank, dstT=dstT, half=half, ep=ep:
                                 e.scalar_tensor_tensor(
                                     out=dstT[:, half * 512:(half + 1) * 512],
                                     in0=bank[:, 0:512], scalar=0.0, in1=oelu[:, ep, :],
                                     op0=ALU.max, op1=ALU.add),
                                 r=[pB, epB], w=[dstB_])
                        else:
                            em.o("scalar", lambda e, bank=bank, dstT=dstT, half=half:
                                 e.activation(out=dstT[:, half * 512:(half + 1) * 512],
                                              in_=bank[:, 0:512], func=AF.Copy),
                                 r=[pB], w=[dstB_])
                first = (tc == 0)
                last = (tc == T // 128 - 1)
                if first:
                    for bi in (6, 7):
                        em.o("tensor", lambda e, bi=bi: e.matmul(
                            pb[bi][:, 0:512], zero_b[0:1, 0:128],
                            zero_b[0:1, 0:512], start=True, stop=False),
                            r=[B["zero_b"]], deps=pbB[bi].rs + [pbB[bi].w],
                            inc=False)
                for h in range(8):
                    kvbank = pb[6 + h // 4]
                    kvpB = pbB[6 + h // 4]
                    mmit = em.o("tensor", lambda e, h=h, kvbank=kvbank, last=last: e.matmul(
                        kvbank[:, (h % 4) * 128:(h % 4) * 128 + 128],
                        ktok[:, h * 128:(h + 1) * 128],
                        vtok[:, h * 128:(h + 1) * 128],
                        start=False, stop=(last and (h % 4) == 3)),
                        r=[B["ktok"], B["vtok"]],
                        inc=True)
                    em.touch(mmit, w=[kvpB])
                for half in range(2):
                    ksbank = pb[5 - half]
                    kspB = pbB[5 - half]
                    mmit = em.o("tensor", lambda e, half=half, ksbank=ksbank,
                                first=first, last=last: e.matmul(
                        ksbank[0:1, 0:512], onescol[:, 0:1],
                        ktok[:, half * 512:(half + 1) * 512],
                        start=first, stop=last),
                        r=[B["ktok"], B["onescol"]],
                        deps=(kspB.rs + [kspB.w] if first else []), inc=True)
                    em.touch(mmit, w=[kspB])
        for h in range(8):
            em.o("scalar", lambda e, h=h: e.activation(
                out=kv_sb[:, h, :], in_=pb[6 + h // 4][:, (h % 4) * 128:(h % 4) * 128 + 128],
                func=AF.Copy), r=[pbB[6 + h // 4]], w=[kvB[h]])
        em.o("vector", lambda e: e.memset(kst_sb[:], 0.0), w=[B["kst_sb"]])
        for h in range(8):
            bank = pb[5 - h // 4]
            col = (h % 4) * 128
            em.o("vector", lambda e, h=h, bank=bank, col=col: e.tensor_copy(
                out=scr_f[0][0:1, 0:128], in_=bank[0:1, col:col + 128]),
                r=[pbB[5 - h // 4]], w=[B["scr_f0"]])
            mmit = em.o("tensor", lambda e: e.transpose(
                pb[0][:, 0:1], scr_f[0][0:1, 0:128], identf[0:1, 0:1]),
                r=[B["scr_f0"], B["identf"]], deps=pbB[0].rs + [pbB[0].w])
            em.touch(mmit, w=[pbB[0]])
            em.o("vector", lambda e, h=h: e.tensor_copy(
                out=kst_sb[:, h:h + 1], in_=pb[0][:, 0:1]),
                r=[pbB[0]], w=[B["kst_sb"]])

    def outer_pass2(e_, wq_sb, wqB, vec_q, kv_sb, kvB, kst_sb, q_sb, yt,
                    zrow, zrow_b, oelu):
        def oload2(ti):
            slot = ti % 2
            xtile, xtB_ = xt[slot], xtB[slot]
            t0 = ti * TILE
            srcap = x0[e_, :, t0:t0 + TILE].rearrange("(c p) t -> p c t", p=128)
            em.o("sync", lambda e, srcap=srcap, xtile=xtile: e.dma_start(
                out=xtile[:, :, 1:TILE + 1], in_=srcap), w=[xtB_],
                dma=("dxa" if slot == 0 else "dxb"))

        oload2(0)
        for ti in range(T // TILE):
            slot = ti % 2
            xtile, xtB_ = xt[slot], xtB[slot]
            t0 = ti * TILE
            if ti + 1 < T // TILE:
                oload2(ti + 1)
            cast_xs(xtile, xtB_)
            emit_stats(lambda c: (xs[:, c, 1:TILE + 1], [B["xs"]]), 2, 3)
            em.o("vector", lambda e: e.tensor_copy(out=rbf_row[:], in_=srow(2)),
                 r=[B["strows"]], w=[B["rbf_row"]])
            rb_bank, rb_pB = bcast_row(rbf_row[0:1, :], [B["rbf_row"]], 2)
            cast_xs(xtile, xtB_, rb_bank, rb_pB)
            for oc in range(NCH):
                bank, pB = pb[oc % 2], pbB[oc % 2]
                g = em.grp()
                for kc in range(NCH):
                    g.mm(lambda e, kc=kc, oc=oc, bank=bank: e.matmul(
                        bank[:, 0:TILE], wq_sb[:, kc, oc * 128:(oc + 1) * 128],
                        xs[:, kc, 0:TILE], start=(kc == 0), stop=False),
                        r=[B["xs"], wqB[(kc, oc // 4)]],
                        first_w=(pB if kc == 0 else None))
                g.close(lambda e, oc=oc, bank=bank: e.matmul(
                    bank[:, 0:TILE], vec_q[0:2, oc * 128:(oc + 1) * 128],
                    r1rows[0:2, 0:TILE], start=False, stop=True),
                    r=[B["vec_q"], B["r1rows"]], w=[pB])
                ep = oc % 2
                epB = B[f"oelu{ep}"]
                em.o("vector", lambda e, bank=bank, ep=ep: e.tensor_scalar_min(
                    out=oelu[:, ep, :], in0=bank[:, 0:TILE], scalar1=0.0),
                    r=[pB], w=[epB])
                em.o("scalar", lambda e, ep=ep: e.activation(
                    out=oelu[:, ep, :], in_=oelu[:, ep, :], func=AF.Exp),
                    r=[epB], w=[epB])
                em.o("vector", lambda e, bank=bank, oc=oc, ep=ep: e.scalar_tensor_tensor(
                    out=q_sb[:, oc, :], in0=bank[:, 0:TILE], scalar=0.0,
                    in1=oelu[:, ep, :], op0=ALU.max, op1=ALU.add),
                    r=[pB, epB], w=[B["q_sb"]])
            for h in range(NCH):
                zp = h % 2
                zr = zrow[0:1, zp * TILE:(zp + 1) * TILE]
                zrb = zrow_b[0:1, zp * TILE:(zp + 1) * TILE]
                zrB = B["zrow"] if zp == 0 else B["zrow_b"]
                sqp, sqpB = scr_sq[zp], B[f"scr_sq{zp}"]
                bank, pB = pb[4 + h % 2], pbB[4 + h % 2]
                mmit = em.o("tensor", lambda e, h=h, bank=bank: e.matmul(
                    bank[0:1, 0:TILE], kst_sb[:, h:h + 1], q_sb[:, h, :],
                    start=True, stop=True), r=[B["kst_sb"], B["q_sb"]],
                    w=[pB])
                em.o("vector", lambda e, bank=bank, zr=zr: e.tensor_scalar(
                    out=zr, in0=bank[0:1, 0:TILE], scalar1=1.0 / T,
                    scalar2=1e-6 / T, op0=ALU.mult, op1=ALU.add),
                    r=[pB], w=[zrB])
                em.o("vector", lambda e, zr=zr: e.reciprocal(out=zr, in_=zr),
                     r=[zrB], w=[zrB])
                em.o("vector", lambda e, zr=zr, zrb=zrb: e.tensor_copy(out=zrb, in_=zr),
                     r=[zrB], w=[zrB])
                zb_bank, zb_pB = bcast_row(zrb, [zrB], 6 + h % 2)
                em.o("vector", lambda e, h=h, zb_bank=zb_bank, sqp=sqp: e.tensor_tensor(
                    out=sqp[:, 0:TILE], in0=q_sb[:, h, :],
                    in1=zb_bank[:, 0:TILE], op=ALU.mult),
                    r=[B["q_sb"], zb_pB], w=[sqpB])
                mm2 = em.o("tensor", lambda e, h=h, bank=bank, sqp=sqp: e.matmul(
                    bank[:, 0:TILE], kv_sb[:, h, :], sqp[:, 0:TILE],
                    start=True, stop=True),
                    r=[kvB[h], sqpB], deps=[pB.w] + pB.rs)
                em.touch(mm2, w=[pB])
                em.o("vector", lambda e, h=h, bank=bank, xtile=xtile: e.tensor_tensor(
                    out=xtile[:, h, 1:TILE + 1], in0=xtile[:, h, 1:TILE + 1],
                    in1=bank[:, 0:TILE], op=ALU.add), r=[pB, xtB_], w=[xtB_])
            cast_xs(xtile, xtB_)
            emit_stats(lambda c: (xs[:, c, 1:TILE + 1], [B["xs"]]), 2, 3)
            em.o("vector", lambda e: e.tensor_copy(out=rbf_row[:], in_=srow(2)),
                 r=[B["strows"]], w=[B["rbf_row"]])
            rb_bank, rb_pB = bcast_row(rbf_row[0:1, :], [B["rbf_row"]], 2)
            em.o("vector", lambda e: e.tensor_copy(out=zrow_b[0:1, 0:TILE], in_=srow(3)),
                 r=[B["strows"]], w=[B["zrow_b"]])
            mub_bank, mub_pB = bcast_row(zrow_b[0:1, 0:TILE], [B["zrow_b"]], 3)
            for c in range(NCH):
                em.o("vector", lambda e, c=c, xtile=xtile: e.tensor_tensor(
                    out=scr_f[c % 2][:, 0:TILE], in0=xtile[:, c, 1:TILE + 1],
                    in1=rb_bank[:, 0:TILE], op=ALU.mult),
                    r=[xtB_, rb_pB], w=[B[f"scr_f{c % 2}"]])
                em.o("vector", lambda e, c=c: e.tensor_tensor(
                    out=yt[:, c, :], in0=scr_f[c % 2][:, 0:TILE],
                    in1=mub_bank[:, 0:TILE], op=ALU.subtract),
                    r=[B[f"scr_f{c % 2}"], mub_pB], w=[B["yt"]])
            tstore(e_, ti, slot, x_dram, xdB)
            em.o("sync", lambda e, t0=t0, e_=e_: e.dma_start(
                out=y_dram[e_, :, t0 + 1:t0 + TILE + 1].rearrange("(c p) t -> p c t", p=128),
                in_=yt[:, :, 0:TILE]), r=[B["yt"]], w=[ydB[e_][ti]], dma=f"dst_y{e_}_{ti}")



    def inner_attn_elem(l, e_, wqkv_sb, wqkvB, wo_sb, Lb, kwin, kwinB, vring,
                        vrB, tabs_raw, tabs_pr, q_sb, p_sb, pT_sb, aout,
                        rstd_b):
        woB = Lb["wo"]

        def ring_ranges(start_slot, nslots):
            out = []
            s = start_slot % 6
            remaining, woff = nslots, 0
            while remaining:
                take = min(remaining, 6 - s)
                out.append((s * 128, woff * 128, take * 128))
                woff += take; remaining -= take; s = 0
            return out

        def tile_head(ti, nobar=False):
            slot = ti % 2
            xtile, xtB_ = xt[slot], xtB[slot]
            tload(e_, ti, slot, y_dram, ydB, nobar=nobar)
            cast_xs(xtile, xtB_, halo=True)
            tb, tbB = tabs_raw[ti % 2], Lb[f"traw{ti % 2}"]
            t0 = ti * TILE
            em.o("gpsimd", lambda e, t0=t0, tb=tb: e.dma_start(
                out=tb[:], in_=tabs_in[:, :, t0:t0 + TILE].rearrange("k p t -> p k t")),
                w=[tbB], dma=f"dtab{ti % 2}")

        tile_head(0, nobar=(e_ == 0))
        for ti in range(NT):
            slot = ti % 2
            par2 = ti % 2
            qbase = par2 * TILE
            qB = Lb[f"q{par2}"]
            w0 = (ti * 4) % 6
            new_ranges = ring_ranges(w0, 4)
            new_slots = [(w0 + k) % 6 for k in range(4)]
            xtile, xtB_ = xt[slot], xtB[slot]
            tb, tbB = tabs_raw[ti % 2], Lb[f"traw{ti % 2}"]
            for k4 in range(4):
                em.o("vector", lambda e, k4=k4, tb=tb: e.tensor_scalar_mul(
                    out=tabs_pr[:, k4, :], in0=tb[:, k4, :],
                    scalar1=qs_col[:, k4:k4 + 1]),
                    r=[tbB, B["qs_col"]], w=[Lb["tabs_pr"]])
            if ti == 0:
                em.o("vector", lambda e: e.memset(kwin[:, :, 512:768], 0.0),
                     w=[kwinB[4], kwinB[5]])
                for sl in (4, 5):
                    em.o("vector", lambda e, sl=sl: e.memset(vring[sl][:], 0.0),
                         w=[vrB[sl]])
            def shsl(kc, off, width):
                base = 1 if kc < 4 else 0
                return xs[:, kc, base + off: base + off + width]
            for side in ("q", "k"):
                colbase = 0 if side == "q" else D
                for oc in range(NCH):
                    bank, pB = pb[oc % 2], pbB[oc % 2]
                    g = em.grp()
                    for kc in range(NCH):
                        fn = lambda e, kc=kc, oc=oc, bank=bank, colbase=colbase: e.matmul(
                            bank[:, 0:TILE],
                            wqkv_sb[:, kc, colbase + oc * 128: colbase + (oc + 1) * 128],
                            shsl(kc, 0, TILE), start=(kc == 0), stop=(kc == NCH - 1))
                        if kc < NCH - 1:
                            g.mm(fn, r=[B["xs"], wqkvB[(kc, (colbase + oc * 128) // 512)]],
                                 first_w=(pB if kc == 0 else None))
                        else:
                            g.close(fn, r=[B["xs"], wqkvB[(kc, (colbase + oc * 128) // 512)]],
                                    w=[pB])
                    if side == "q":
                        em.o("scalar", lambda e, oc=oc, bank=bank, qbase=qbase:
                             e.activation(
                            out=q_sb[:, oc, qbase:qbase + TILE], in_=bank[:, 0:TILE],
                            func=AF.Copy), r=[pB], w=[qB])
                    else:
                        for (rc, to, wd) in new_ranges:
                            wslots = [kwinB[rc // 128 + k] for k in range(wd // 128)]
                            em.o("scalar", lambda e, oc=oc, bank=bank, rc=rc, to=to, wd=wd:
                                 e.activation(
                                out=kwin[:, oc, rc:rc + wd], in_=bank[:, to:to + wd],
                                func=AF.Copy), r=[pB], w=wslots)
            pss, pssB = pb[4], pbB[4]
            nmm = 0
            for side in ("q", "k"):
                for c in range(NCH):
                    sq, sqB = scr_sq[c % 2], B[f"scr_sq{c % 2}"]
                    first = (nmm == 0)
                    if side == "q":
                        em.o("scalar", lambda e, c=c, sq=sq, qbase=qbase: e.activation(
                            out=sq[:, 0:TILE], in_=q_sb[:, c, qbase:qbase + TILE],
                            func=AF.Square), r=[qB], w=[sqB])
                    else:
                        for (rc, to, wd) in new_ranges:
                            rslots = [kwinB[rc // 128 + k] for k in range(wd // 128)]
                            em.o("scalar", lambda e, c=c, sq=sq, rc=rc, to=to, wd=wd:
                                 e.activation(
                                out=sq[:, to:to + wd], in_=kwin[:, c, rc:rc + wd],
                                func=AF.Square),
                                r=rslots, w=[sqB])
                    last = (nmm == 15)
                    mbase = 0 if side == "q" else 8
                    it = em.o("tensor", lambda e, c=c, sq=sq, mbase=mbase,
                              first=first, last=last: e.matmul(
                        pss[0:32, 0:TILE], selss[:, mbase + c, :], sq[:, 0:TILE],
                        start=first, stop=last),
                        r=[sqB, B["selss"]],
                        deps=(pssB.rs + [pssB.w] if first else []))
                    if last:
                        em.touch(it, w=[pssB])
                    nmm += 1
            em.o("scalar", lambda e: e.activation(
                out=rstd_sb[:, 0:TILE], in_=pss[0:32, 0:TILE], func=AF.Ln,
                bias=cvals[0:32, 2:3]),
                r=[pssB, B["cvals"]], w=[B["rstd_sb"]])
            em.o("scalar", lambda e: e.activation(
                out=rstd_sb[:, 0:TILE], in_=rstd_sb[:, 0:TILE], func=AF.Exp,
                scale=-0.5, bias=bias32[:, 0:1]),
                r=[B["rstd_sb"], B["bias32"]], w=[B["rstd_sb"]])
            em.o("vector", lambda e: e.tensor_scalar(
                out=rstd_sb[:, 0:TILE], in0=rstd_sb[:, 0:TILE],
                scalar1=bias32[:, 1:2], scalar2=None, op0=ALU.min),
                r=[B["rstd_sb"], B["bias32"]], w=[B["rstd_sb"]])
            em.o("vector", lambda e: e.tensor_copy(out=rstd_b[:], in_=rstd_sb[:, 0:TILE]),
                 r=[B["rstd_sb"]], w=[Lb["rstd_b"]])
            rTB = Lb[f"rstdT{par2}"]
            for s in range(4):
                it = em.o("tensor", lambda e, s=s: e.transpose(
                    pb[5][:, 0:32], rstd_sb[:, s * 128:(s + 1) * 128],
                    identf[0:32, 0:32]), r=[B["rstd_sb"], B["identf"]],
                    deps=pbB[5].rs + [pbB[5].w])
                em.touch(it, w=[pbB[5]])
                em.o("vector", lambda e, s=s, par2=par2: e.tensor_copy(
                    out=rstdT[:, par2 * 128 + s * 32: par2 * 128 + (s + 1) * 32],
                    in_=pb[5][:, 0:32]),
                    r=[pbB[5]], w=[rTB])
            for c in range(NCH):
                em.o("vector", lambda e, c=c, qbase=qbase: e.stream_shuffle(
                    out=scr_sq[0][:, 0:TILE], in_=q_sb[:, c, qbase:qbase + TILE],
                    mask=SHUF32), r=[qB], w=[B["scr_sq0"]])
                em.o("vector", lambda e, c=c, qbase=qbase: e.tensor_tensor(
                    out=scr_sq[1][:, 0:TILE], in0=q_sb[:, c, qbase:qbase + TILE],
                    in1=tabs_pr[:, 0, :], op=ALU.mult),
                    r=[qB, Lb["tabs_pr"]], w=[B["scr_sq1"]])
                em.o("vector", lambda e: e.tensor_tensor(
                    out=scr_sq[0][:, 0:TILE], in0=scr_sq[0][:, 0:TILE],
                    in1=tabs_pr[:, 1, :], op=ALU.mult),
                    r=[B["scr_sq0"], Lb["tabs_pr"]], w=[B["scr_sq0"]])
                em.o("vector", lambda e, c=c, qbase=qbase: e.tensor_tensor(
                    out=q_sb[:, c, qbase:qbase + TILE], in0=scr_sq[1][:, 0:TILE],
                    in1=scr_sq[0][:, 0:TILE], op=ALU.add),
                    r=[B["scr_sq0"], B["scr_sq1"]], w=[qB])
                it = em.o("tensor", lambda e, c=c: e.matmul(
                    pb[5][:, 0:TILE], sel2k[:, c, :], rstd_b[:],
                    start=True, stop=True), r=[B["sel2k"], Lb["rstd_b"]],
                    deps=pbB[5].rs + [pbB[5].w])
                em.touch(it, w=[pbB[5]])
                for (rc, to, wd) in new_ranges:
                    kB_ = kwinB[rc // 128]
                    kslots = [kwinB[rc // 128 + k] for k in range(wd // 128)]
                    em.o("vector", lambda e, c=c, rc=rc, wd=wd: e.stream_shuffle(
                        out=scr_sq[0][:, 0:wd], in_=kwin[:, c, rc:rc + wd],
                        mask=SHUF32), r=kslots, w=[B["scr_sq0"]])
                    em.o("vector", lambda e, c=c, rc=rc, to=to, wd=wd: e.tensor_tensor(
                        out=scr_sq[1][:, 0:wd], in0=kwin[:, c, rc:rc + wd],
                        in1=tabs_pr[:, 2, to:to + wd], op=ALU.mult),
                        r=kslots + [Lb["tabs_pr"]], w=[B["scr_sq1"]])
                    em.o("vector", lambda e, to=to, wd=wd: e.tensor_tensor(
                        out=scr_sq[0][:, 0:wd], in0=scr_sq[0][:, 0:wd],
                        in1=tabs_pr[:, 3, to:to + wd], op=ALU.mult),
                        r=[B["scr_sq0"], Lb["tabs_pr"]], w=[B["scr_sq0"]])
                    em.o("vector", lambda e, wd=wd: e.tensor_tensor(
                        out=scr_sq[1][:, 0:wd], in0=scr_sq[1][:, 0:wd],
                        in1=scr_sq[0][:, 0:wd], op=ALU.add),
                        r=[B["scr_sq0"], B["scr_sq1"]], w=[B["scr_sq1"]])
                    em.o("vector", lambda e, c=c, rc=rc, to=to, wd=wd: e.tensor_tensor(
                        out=kwin[:, c, rc:rc + wd], in0=scr_sq[1][:, 0:wd],
                        in1=pb[5][:, to:to + wd], op=ALU.mult),
                        r=[B["scr_sq1"], pbB[5]], w=kslots)
            for ts in range(4):
                gslot = (ti * 4 + ts) % 6
                for half in range(2):
                    bank, pB = pb[2 + ts % 2], pbB[2 + ts % 2]
                    g = em.grp()
                    for kc in range(NCH):
                        fn = lambda e, kc=kc, half=half, ts=ts, bank=bank: e.matmul(
                            bank[:, 0:512], shsl(kc, ts * 128, 128),
                            wqkv_sb[:, kc, 2 * D + half * 512: 2 * D + (half + 1) * 512],
                            start=(kc == 0), stop=(kc == NCH - 1))
                        rr = [B["xs"], wqkvB[(kc, (2 * D + half * 512) // 512)]]
                        if kc < NCH - 1:
                            g.mm(fn, r=rr, first_w=(pB if kc == 0 else None))
                        else:
                            g.close(fn, r=rr, w=[pB])
                    em.o("scalar", lambda e, gslot=gslot, half=half, bank=bank:
                         e.activation(out=vring[gslot][:, half * 512:(half + 1) * 512],
                                      in_=bank[:, 0:512], func=AF.Copy),
                         r=[pB], w=[vrB[gslot]])
            if ti + 1 < NT:
                tile_head(ti + 1)
            for s in range(4):
                s_glob = ti * 4 + s
                mvar = 2 if s_glob == 0 else (1 if s_glob == 1 else 0)
                win_ranges = ring_ranges((w0 + s - 2) % 6, 3)
                win_slots = [((w0 + s - 2) + k) % 6 for k in range(3)]
                for bi in (6, 7):
                    em.o("tensor", lambda e, bi=bi: e.matmul(
                        pb[bi][:, 0:512], zero_b[0:1, 0:128], zero_b[0:1, 0:512],
                        start=True, stop=False),
                        r=[B["zero_b"]], deps=pbB[bi].rs + [pbB[bi].w], inc=False)

                def st_sims(c):
                    for hh in range(2):
                        bank, pB = pb[2 + (c % 2) * 2 + hh], pbB[2 + (c % 2) * 2 + hh]
                        nr = len(win_ranges)
                        for ri, (rc, to, wd) in enumerate(win_ranges):
                            it = em.o("tensor", lambda e, c=c, s=s, hh=hh, bank=bank,
                                      rc=rc, to=to, wd=wd, ri=ri, nr=nr, qbase=qbase:
                                      e.matmul(bank[:, to:to + wd],
                                               q_sb[hh * 64:(hh + 1) * 64, c,
                                                    qbase + s * 128:qbase + (s + 1) * 128],
                                               kwin[hh * 64:(hh + 1) * 64, c, rc:rc + wd],
                                               start=(ri == 0), stop=(ri == nr - 1)),
                                      deps=[qB.w, kwinB[rc // 128].w]
                                           + (pB.rs + [pB.w] if ri == 0 else []),
                                      inc=(ri == nr - 1))
                            if ri == nr - 1:
                                em.touch(it, w=[pB],
                                         r=[qB] + [kwinB[sl] for sl in win_slots])

                def st_mask(c):
                    for hh in range(2):
                        ix = (c % 4) * 2 + hh
                        bank, pB = pb[2 + (c % 2) * 2 + hh], pbB[2 + (c % 2) * 2 + hh]
                        em.o("vector", lambda e, c=c, s=s, ix=ix, bank=bank,
                             mvar=mvar, par2=par2: e.scalar_tensor_tensor(
                                 out=bank[:, 0:384], in0=bank[:, 0:384],
                                 scalar=rstdT[:, par2 * 128 + s * 32 + c * 2 + (ix % 2):
                                              par2 * 128 + s * 32 + c * 2 + (ix % 2) + 1],
                                 in1=masks[:, mvar, :], op0=ALU.mult, op1=ALU.add),
                             r=[rTB, B["masks"], pB], w=[pB])
                        em.o("vector", lambda e, ix=ix, bank=bank: e.reduce_max(
                            out=nm_sb[:, ix:ix + 1], in_=bank[:, 0:384],
                            axis=AX.X, negate=True),
                            r=[pB], w=[B[f"nm{ix}"]])

                def st_exp(c):
                    for hh in range(2):
                        ix = (c % 4) * 2 + hh
                        bank, pB = pb[2 + (c % 2) * 2 + hh], pbB[2 + (c % 2) * 2 + hh]
                        em.o("scalar", lambda e, ix=ix, bank=bank: e.activation(
                            out=p_sb[:, ix, :], in_=bank[:, 0:384], func=AF.Exp,
                            bias=nm_sb[:, ix:ix + 1], accum_out=l_sb[:, ix:ix + 1]),
                            r=[pB, B[f"nm{ix}"]],
                            w=[Lb[f"p{ix}"], B[f"l{ix}"]])

                def st_tail1(c):
                    for hh in range(2):
                        ix = (c % 4) * 2 + hh
                        em.o("vector", lambda e, ix=ix: e.reciprocal(
                            out=linv_sb[:, ix:ix + 1], in_=l_sb[:, ix:ix + 1]),
                            r=[B[f"l{ix}"]], w=[B[f"li{ix}"]])
                        em.o("vector", lambda e, ix=ix: e.tensor_scalar_mul(
                            out=p_sb[:, ix, :], in0=p_sb[:, ix, :],
                            scalar1=linv_sb[:, ix:ix + 1]),
                            r=[B[f"li{ix}"], Lb[f"p{ix}"]], w=[Lb[f"p{ix}"]])

                def st_tail2(c):
                    for hh in range(2):
                        h = 2 * c + hh
                        ix = (c % 4) * 2 + hh
                        tbank, tpB = pb[hh], pbB[hh]
                        tview = tbank[:].bitcast(BF16)
                        for kc3 in range(3):
                            it = em.o("tensor", lambda e, ix=ix, kc3=kc3, tview=tview:
                                      e.transpose(
                                tview[:, kc3 * 128:(kc3 + 1) * 128],
                                p_sb[:, ix, kc3 * 128:(kc3 + 1) * 128], identb[:]),
                                r=[Lb[f"p{ix}"], B["identb"]],
                                deps=(tpB.rs + [tpB.w] if kc3 == 0 else []),
                                inc=(kc3 == 2))
                            if kc3 == 2:
                                em.touch(it, w=[tpB])
                        if (c + hh) % 2 == 0:
                            em.o("scalar", lambda e, ix=ix, tview=tview: e.activation(
                                out=pT_sb[:, ix, :], in_=tview[:, 0:384], func=AF.Copy),
                                r=[tpB], w=[Lb[f"pT{ix}"]])
                        else:
                            em.o("vector", lambda e, ix=ix, tview=tview: e.tensor_copy(
                                out=pT_sb[:, ix, :], in_=tview[:, 0:384]),
                                r=[tpB], w=[Lb[f"pT{ix}"]])
                        avbank = pb[6 + c // 4]
                        avpB = pbB[6 + c // 4]
                        hp_col = (c % 4) * 128
                        base = hh * 64
                        av_slots = []
                        for kc3 in range(3):
                            gslot = (ti * 4 + s - 2 + kc3) % 6
                            av_slots.append(vrB[gslot])
                            it = em.o("tensor", lambda e, ix=ix, kc3=kc3,
                                      gslot=gslot, avbank=avbank, hp_col=hp_col,
                                      base=base, h=h: e.matmul(
                                avbank[base:base + 64, hp_col:hp_col + 128],
                                vring[gslot][:, h * 64:(h + 1) * 64],
                                pT_sb[:, ix, kc3 * 128:(kc3 + 1) * 128],
                                start=False, stop=False,
                                tile_position=(0, base)),
                                deps=[vrB[gslot].w, Lb[f"pT{ix}"].w],
                                inc=(kc3 == 2))
                            if kc3 == 2:
                                em.touch(it, w=[avpB], r=av_slots + [Lb[f"pT{ix}"]])

                for c in range(NCH):
                    st_sims(c)
                    st_mask(c)
                    if c > 2:
                        st_tail2(c - 3)
                    st_exp(c)
                    if c > 1:
                        st_tail1(c - 2)
                st_tail1(NCH - 2)
                st_tail2(NCH - 3)
                st_tail1(NCH - 1)
                st_tail2(NCH - 2)
                st_tail2(NCH - 1)
                for bi in (6, 7):
                    it = em.o("tensor", lambda e, bi=bi: e.matmul(
                        pb[bi][:, 0:512], zero_b[0:1, 0:128], zero_b[0:1, 0:512],
                        start=False, stop=True),
                        r=[B["zero_b"]], inc=True)
                    em.touch(it, w=[pbB[bi]])
                for bi in (6, 7):
                    em.o("vector", lambda e, bi=bi, s=s: e.tensor_copy(
                        out=aout[:, (bi - 6) * 4:(bi - 6) * 4 + 4,
                                 s * 128:(s + 1) * 128],
                        in_=pb[bi][:, 0:512].rearrange("p (c t) -> p c t", c=4)),
                        r=[pbB[bi]], w=[Lb["aout"]])
            for oc in range(NCH):
                bank, pB = pb[oc % 2], pbB[oc % 2]
                g = em.grp()
                for ic in range(NCH):
                    fn = lambda e, ic=ic, oc=oc, bank=bank: e.matmul(
                        bank[:, 0:TILE], wo_sb[:, ic, oc * 128:(oc + 1) * 128],
                        aout[:, ic, :], start=(ic == 0), stop=(ic == NCH - 1))
                    if ic < NCH - 1:
                        g.mm(fn, r=[Lb["aout"], woB], first_w=(pB if ic == 0 else None))
                    else:
                        g.close(fn, r=[Lb["aout"], woB], w=[pB])
                em.o("vector", lambda e, oc=oc, bank=bank, xtile=xtile: e.tensor_tensor(
                    out=xtile[:, oc, 1:TILE + 1], in0=xtile[:, oc, 1:TILE + 1],
                    in1=bank[:, 0:TILE], op=ALU.add), r=[pB, xtB_], w=[xtB_])
            tstore(e_, ti, slot, y_dram, ydB)

    def inner_ffn_elem(l, e_, w1_sb, w1B, w2_sb, w2B, vecw1, Fb, b2row, hsb, gsb):
        tload(e_, 0, 0, y_dram, ydB, nobar=(e_ == 0))
        for ti in range(NT):
            slot = ti % 2
            xtile, xtB_ = xt[slot], xtB[slot]
            cast_xs(xtile, xtB_, halo=True)
            def ssrc(c):
                sl = xs[:, c, 1:TILE + 1] if c < 4 else xs[:, c, 0:TILE]
                return (sl, [B["xs"]])
            emit_stats(ssrc, 0, 1)
            rb_bank, rb_pB = bcast_row(rbf_row[0:1, :], [B["rbf_row"]], 0)
            cast_xs(xtile, xtB_, rb_bank, rb_pB, shifted=True)
            if ti + 1 < NT:
                tload(e_, ti + 1, (ti + 1) % 2, y_dram, ydB)
            for u in range(2):
                u0 = u * 256
                opened = [False]
                def open_acc_banks():
                    for bi in range(4, 8):
                        em.o("tensor", lambda e, bi=bi: e.matmul(
                            pb[bi][:, 0:512], zero_b[0:1, 0:128], zero_b[0:1, 0:512],
                            start=True, stop=False),
                            r=[B["zero_b"]], deps=pbB[bi].rs + [pbB[bi].w], inc=False)
                    opened[0] = True
                def st_w2(j):
                    jw = 128 if j < NJ - 1 else DI - 128 * (NJ - 1)
                    par = j % 2
                    hbuf, hB = hsb[par], Fb[f"h{par}"]
                    w2reads = []
                    for oc in range(NCH):
                        bank = pb[4 + oc // 2]
                        col = (oc % 2) * 256
                        w2reads.append(w2B[(j, oc // 4)])
                        it = em.o("tensor", lambda e, j=j, oc=oc, jw=jw, hbuf=hbuf,
                                  bank=bank, col=col: e.matmul(
                            bank[:, col:col + 256], w2_sb[0:jw, j, oc * 128:(oc + 1) * 128],
                            hbuf[0:jw, :], start=False, stop=False),
                            deps=[w2B[(j, oc // 4)].w, hB.w],
                            inc=(oc == NCH - 1))
                        if oc == NCH - 1:
                            em.touch(it, r=w2reads + [hB],
                                     w=[pbB[4], pbB[5], pbB[6], pbB[7]])

                for j in range(NJ):
                    jw = 128 if j < NJ - 1 else DI - 128 * (NJ - 1)
                    par = j % 2
                    hbuf, hB = hsb[par], Fb[f"h{par}"]
                    gbuf, gB_ = gsb[par], Fb[f"gelu{par}"]
                    gate_b, a_b = (2, 3) if par == 0 else (0, 1)
                    for half, bank_i in ((1, gate_b), (0, a_b)):
                        bank, pB = pb[bank_i], pbB[bank_i]
                        cbase = half * DI + j * 128
                        g = em.grp()
                        for kc in range(NCH):
                            fn = lambda e, kc=kc, cbase=cbase, jw=jw, bank=bank, u0=u0: \
                                e.matmul(bank[0:jw, 0:256],
                                         w1_sb[:, kc, cbase:cbase + jw],
                                         xs[:, kc, u0:u0 + 256],
                                         start=(kc == 0), stop=False)
                            g.mm(fn, r=[B["xs"], w1B[(kc, cbase // 512)]],
                                 first_w=(pB if kc == 0 else None))
                        g.close(lambda e, cbase=cbase, jw=jw, bank=bank, u0=u0:
                                e.matmul(bank[0:jw, 0:256],
                                         vecw1[0:2, cbase:cbase + jw],
                                         r1rows[0:2, u0:u0 + 256],
                                         start=False, stop=True),
                                r=[Fb["vecw1"], B["r1rows"]], w=[pB])
                    em.o("scalar", lambda e, jw=jw, gbuf=gbuf, gate_b=gate_b:
                         e.activation(
                        out=gbuf[0:jw, 0:256], in_=pb[gate_b][0:jw, 0:256],
                        func=AF.Gelu),
                        r=[pbB[gate_b]], w=[gB_])
                    em.o("vector", lambda e, jw=jw, hbuf=hbuf, gbuf=gbuf, a_b=a_b:
                         e.scalar_tensor_tensor(
                        out=hbuf[0:jw, :], in0=pb[a_b][0:jw, 0:256], scalar=1.0,
                        in1=gbuf[0:jw, 0:256], op0=ALU.mult, op1=ALU.mult),
                        r=[pbB[a_b], gB_], w=[hB])
                    if not opened[0]:
                        open_acc_banks()
                    if j > 0:
                        st_w2(j - 1)
                st_w2(NJ - 1)
                for oc in range(NCH):
                    bank = pb[4 + oc // 2]
                    bB = pbB[4 + oc // 2]
                    col = (oc % 2) * 256
                    it = em.o("tensor", lambda e, oc=oc, bank=bank, col=col,
                              last=(oc % 2 == 1): e.matmul(
                        bank[:, col:col + 256], b2row[0:1, oc * 128:(oc + 1) * 128],
                        onesrow[0:1, 0:256], start=False, stop=last),
                        r=[Fb["b2row"], B["onesrow"]], inc=True)
                    em.touch(it, w=[bB])
                for oc in range(NCH):
                    bank, bB = pb[4 + oc // 2], pbB[4 + oc // 2]
                    col = (oc % 2) * 256
                    em.o("vector", lambda e, oc=oc, bank=bank, col=col, u0=u0,
                         xtile=xtile: e.tensor_tensor(
                        out=xtile[:, oc, 1 + u0:1 + u0 + 256],
                        in0=xtile[:, oc, 1 + u0:1 + u0 + 256],
                        in1=bank[:, col:col + 256], op=ALU.add),
                        r=[bB, xtB_], w=[xtB_])
            tstore(e_, ti, slot, y_dram, ydB)

    def inner_layer(l):
        with contextlib.ExitStack() as lctx:
            lsb = lambda name, shape, dt: lctx.enter_context(
                nc.sbuf_tensor(f"{name}_l{l}", shape, dt))
            with contextlib.ExitStack() as actx:
                asb = lambda name, shape, dt: actx.enter_context(
                    nc.sbuf_tensor(f"{name}_l{l}", shape, dt))
                wqkv_sb = asb("wqkv", [128, NCH, 3 * D], BF16)
                wo_sb = asb("wo", [128, NCH, D], BF16)
                kwin = asb("kwin", [128, NCH, 768], BF16)
                vring = [asb(f"vring{i}", [128, D], BF16) for i in range(6)]
                tabs_raw = [asb(f"traw{i}", [128, 4, TILE], BF16) for i in range(2)]
                tabs_pr = asb("tpr", [128, 4, TILE], BF16)
                q_sb = asb("q", [128, NCH, 2 * TILE], BF16)
                p_sb = asb("p", [128, 8, 384], BF16)
                pT_sb = asb("pT", [128, 8, 384], BF16)
                aout = asb("aout", [128, NCH, TILE], BF16)
                rstd_b = asb("rstdb", [32, TILE], BF16)
                astg = [asb(f"astg{i}", [128, 512], F32) for i in range(2)]
                stage = scr_f + astg
                Lb = {n: TBuf() for n in
                      ("wo", "tabs_pr", "aout", "rstd_b",
                       "traw0", "traw1", "q0", "q1", "rstdT0", "rstdT1")}
                kwinB = [TBuf() for _ in range(6)]
                for ix in range(8):
                    Lb[f"p{ix}"] = TBuf()
                    Lb[f"pT{ix}"] = TBuf()
                    B[f"nm{ix}"] = TBuf(); B[f"l{ix}"] = TBuf()
                    B[f"li{ix}"] = TBuf()
                vrB = [TBuf() for _ in range(6)]
                wqkvB = {(kc, g): TBuf() for kc in range(NCH) for g in range(6)}
                woB = {(kc, g): TBuf() for kc in range(NCH) for g in range(2)}
                stageB = [B["scr_f0"], B["scr_f1"], TBuf(), TBuf()]

                em.o("sync", lambda e: e.dma_start(
                    out=qs_col[:], in_=qsks[l].rearrange("k p -> p k"),
                    allow_slow_non_contiguous=True),
                    w=[B["qs_col"]], dma="d_init")

                stream_convert(wqkv_sb, wqkvB, wqkv[l], NCH, 3 * D,
                               stage, stageB, "wqkv")
                stream_convert(wo_sb, {(kc, g): Lb["wo"] for kc in range(NCH)
                                       for g in range(2)},
                               wo[l], NCH, D, stage, stageB, "wo")

                for e_ in range(NE):
                    inner_attn_elem(l, e_, wqkv_sb, wqkvB, wo_sb, Lb, kwin,
                                    kwinB, vring, vrB, tabs_raw, tabs_pr, q_sb,
                                    p_sb, pT_sb, aout, rstd_b)
                em.phase_barrier()
                em.flush()

            with contextlib.ExitStack() as fctx:
                fsb = lambda name, shape, dt: fctx.enter_context(
                    nc.sbuf_tensor(f"{name}_l{l}", shape, dt))
                w1_sb = fsb("w1", [128, NCH, 2 * DI], BF16)
                w2_sb = fsb("w2", [128, NJ, D], BF16)
                vecw1 = fsb("vecw1", [2, 2 * DI], BF16)
                b2row = fsb("b2row", [1, D], BF16)
                glb = fsb("glb", [128, 2, NCH], BF16)
                gvec = fsb("gvec", [128, NCH], F32)
                vrow = rbf_row
                hsb = [fsb(f"h{i}", [128, 256], BF16) for i in range(2)]
                gsb = scr_sq
                stage = scr_f
                Fb = {n: TBuf() for n in ("vecw1", "b2row", "glb", "gvec",
                                          "h0", "h1")}
                Fb["vrow"] = B["rbf_row"]
                Fb["gelu0"] = B["scr_sq0"]
                Fb["gelu1"] = B["scr_sq1"]
                w1B = {(kc, g): TBuf() for kc in range(NCH)
                       for g in range((2 * DI + 511) // 512)}
                w2B = {(j, g): TBuf() for j in range(NJ) for g in range(2)}
                stageB = [B["scr_f0"], B["scr_f1"]]

                em.o("sync", lambda e: e.dma_start(
                    out=gvec[:], in_=lng[l].rearrange("(c p) -> p c", p=128),
                    allow_slow_non_contiguous=True),
                    w=[Fb["gvec"]], dma="d_init")
                em.o("gpsimd", lambda e: e.dma_start(
                    out=stage[0][:, 0:NCH],
                    in_=lnb[l].rearrange("(c p) -> p c", p=128),
                    allow_slow_non_contiguous=True),
                    w=[stageB[0]], dma="dsg0")
                em.o("vector", lambda e: e.tensor_copy(
                    out=glb[:, 1, :], in_=stage[0][:, 0:NCH]),
                    r=[stageB[0]], w=[Fb["glb"]])
                em.o("vector", lambda e: e.tensor_copy(
                    out=glb[:, 0, :], in_=gvec[:]), r=[Fb["gvec"]], w=[Fb["glb"]])

                ncols = 2 * DI
                for c0 in range(0, ncols, 512):
                    c1 = min(ncols, c0 + 512)
                    for kc in range(NCH):
                        st, stB = stage[kc % 2], stageB[kc % 2]
                        em.o("gpsimd", lambda e, kc=kc, c0=c0, c1=c1, st=st:
                             e.dma_start(out=st[:, 0:c1 - c0],
                                         in_=w1[l, kc * 128:(kc + 1) * 128, c0:c1]),
                             w=[stB], dma=f"dsg{kc % 2}")
                        sc_b = scr_sq[kc % 2]
                        scB = B[f"scr_sq{kc % 2}"]
                        em.o("vector", lambda e, c0=c0, c1=c1, st=st, sc_b=sc_b:
                             e.tensor_copy(out=sc_b[:, 0:c1 - c0], in_=st[:, 0:c1 - c0]),
                             r=[stB], w=[scB])
                        itc = em.o("tensor", lambda e, kc=kc, c0=c0, c1=c1, sc_b=sc_b:
                                   e.matmul(pb[0][0:1, 0:c1 - c0],
                                            glb[:, 0, kc:kc + 1], sc_b[:, 0:c1 - c0],
                                            start=(kc == 0), stop=(kc == NCH - 1)),
                                   r=[scB, Fb["glb"]],
                                   deps=(pbB[0].rs + [pbB[0].w] if kc == 0 else []))
                        itv = em.o("tensor", lambda e, kc=kc, c0=c0, c1=c1, sc_b=sc_b:
                                   e.matmul(pb[1][0:1, 0:c1 - c0],
                                            glb[:, 1, kc:kc + 1], sc_b[:, 0:c1 - c0],
                                            start=(kc == 0), stop=(kc == NCH - 1)),
                                   r=[scB, Fb["glb"]],
                                   deps=(pbB[1].rs + [pbB[1].w] if kc == 0 else []))
                        if kc == NCH - 1:
                            em.touch(itc, w=[pbB[0]])
                            em.touch(itv, w=[pbB[1]])
                        em.o("vector", lambda e, kc=kc, c0=c0, c1=c1, st=st:
                             e.tensor_scalar_mul(out=w1_sb[:, kc, c0:c1],
                                                 in0=st[:, 0:c1 - c0],
                                                 scalar1=gvec[:, kc:kc + 1]),
                             r=[stB, Fb["gvec"]], w=[w1B[(kc, c0 // 512)]])
                    em.o("vector", lambda e, c0=c0, c1=c1: e.tensor_copy(
                        out=vecw1[0:1, c0:c1], in_=pb[0][0:1, 0:c1 - c0]),
                        r=[pbB[0]], w=[Fb["vecw1"]])
                    st, stB = stage[0], stageB[0]
                    em.o("gpsimd", lambda e, c0=c0, c1=c1, st=st: e.dma_start(
                        out=st[0:1, 0:c1 - c0], in_=b1[l:l + 1, c0:c1]),
                        w=[stB], dma="dsg0")
                    em.o("vector", lambda e, c0=c0, c1=c1, st=st: e.tensor_tensor(
                        out=vrow[0:1, 0:c1 - c0], in0=pb[1][0:1, 0:c1 - c0],
                        in1=st[0:1, 0:c1 - c0], op=ALU.add),
                        r=[pbB[1], stB], w=[Fb["vrow"]])
                    em.o("gpsimd", lambda e, c0=c0, c1=c1: e.dma_start(
                        out=vecw1[1:2, c0:c1], in_=vrow[0:1, 0:c1 - c0]),
                        r=[Fb["vrow"]], w=[Fb["vecw1"]], dma="dvec")

                for g in range(2):
                    em.o("gpsimd", lambda e, g=g: e.dma_start(
                        out=stage[1][0:1, 0:512],
                        in_=b2[l:l + 1, g * 512:(g + 1) * 512]),
                        w=[stageB[1]], dma="dsg1")
                    em.o("vector", lambda e, g=g: e.tensor_copy(
                        out=b2row[0:1, g * 512:(g + 1) * 512],
                        in_=stage[1][0:1, 0:512]),
                        r=[stageB[1]], w=[Fb["b2row"]])

                for j in range(NJ):
                    kw = 128 if j < NJ - 1 else DI - 128 * (NJ - 1)
                    for g in range(2):
                        st, stB = stage[j % 2], stageB[j % 2]
                        em.o("gpsimd", lambda e, j=j, g=g, kw=kw, st=st: e.dma_start(
                            out=st[0:kw, 0:512],
                            in_=w2[l, j * 128:j * 128 + kw, g * 512:(g + 1) * 512]),
                            w=[stB], dma=f"dsg{j % 2}")
                        em.o("vector", lambda e, j=j, g=g, kw=kw, st=st:
                             e.tensor_copy(out=w2_sb[0:kw, j, g * 512:(g + 1) * 512],
                                           in_=st[0:kw, 0:512]),
                             r=[stB], w=[w2B[(j, g)]])

                for e_ in range(NE):
                    inner_ffn_elem(l, e_, w1_sb, w1B, w2_sb, w2B, vecw1, Fb,
                                   b2row, hsb, gsb)
                em.phase_barrier()
                em.flush()

    init_phase()
    em.o("vector", lambda e: e.memset(zero_b[:], 0.0), w=[B["zero_b"]])
    em.flush()
    outer_block()
    for l in range(NL):
        inner_layer(l)
    final_items = []
    for e_ in range(NE):
        for ti in range(NT):
            xtile, xtB_ = tload(e_, ti, ti % 2, x_dram, xdB)
            ytile2, ytB2 = tload(e_, ti, (ti + 1) % 2, y_dram, ydB)
            for c in range(NCH):
                em.o("vector", lambda e, c=c, xtile=xtile, ytile2=ytile2: e.tensor_tensor(
                    out=xtile[:, c, 1:TILE + 1], in0=xtile[:, c, 1:TILE + 1],
                    in1=ytile2[:, c, 1:TILE + 1], op=ALU.add),
                    r=[xtB_, ytB2], w=[xtB_])
            t0 = ti * TILE
            it = em.o("sync", lambda e, t0=t0, e_=e_, xtile=xtile: e.dma_start(
                out=yout[e_, :, t0:t0 + TILE].rearrange("(c p) t -> p c t", p=128),
                in_=xtile[:, :, 1:TILE + 1]), r=[xtB_], dma="d_out")
            final_items.append(it)
    em.o("gpsimd", lambda e: e.nop(), deps=[final_items[-1]])
    em.flush()
    ctx.close()
    return nc



def _host_constants(T):
    tabs = _rope_tables(T).astype(BF_NP)
    masks = _masks().astype(BF_NP)
    identf = np.eye(128, dtype=np.float32)
    bias32 = np.zeros((32, 2), dtype=np.float32)
    bias32[0:16, 0] = np.log(8.0)
    bias32[0:16, 1] = 8e12
    bias32[16:32, 1] = 1e12
    selss = _sel_ss().astype(BF_NP)
    sel2k = _sel2k().astype(BF_NP)
    selst = _sel_stat().astype(BF_NP)
    return dict(tabs=tabs, masks=masks, identf=identf, bias32=bias32,
                selss=selss, sel2k=sel2k, selst=selst)


def kernel(**inputs):
    T, NL, NE, NCORE = 4096, 6, 2, 8
    x = np.asarray(inputs["L0_attn_x"])
    colp = _colperm_qkv()
    wqkv = np.ascontiguousarray(np.asarray(inputs["L0_ffn_Wqkv"])[:, :, colp])
    qs = np.asarray(inputs["L0_ffn_q_scale"])
    ks = np.asarray(inputs["L0_ffn_k_scale"])
    qsks = np.zeros((NL, 4, 128), dtype=np.float32)
    for l in range(NL):
        qp = qs[l][SIG]; kp = ks[l][SIG]
        qsks[l, 0, :] = np.concatenate([qp, qp])
        qsks[l, 2, :] = np.concatenate([kp, kp])
        qsks[l, 1, :] = qsks[l, 0, SHUF_FULL]
        qsks[l, 3, :] = qsks[l, 2, SHUF_FULL]
    consts = _host_constants(T)
    common = dict(
        wq=np.ascontiguousarray(inputs["L0_attn_Wq"]),
        wk=np.ascontiguousarray(inputs["L0_attn_Wk"]),
        wv=np.ascontiguousarray(inputs["L0_attn_Wv"]),
        bqv=np.ascontiguousarray(np.stack([inputs["L0_attn_bq"],
                                           inputs["L0_attn_bk"],
                                           inputs["L0_attn_bv"]])),
        wqkv=wqkv,
        wo=np.ascontiguousarray(inputs["L0_ffn_Wo"]),
        w1=np.ascontiguousarray(inputs["L0_ffn_W1"]),
        b1=np.ascontiguousarray(inputs["L0_ffn_b1"]),
        w2=np.ascontiguousarray(inputs["L0_ffn_W2"]),
        b2=np.ascontiguousarray(inputs["L0_ffn_b2"]),
        lng=np.ascontiguousarray(inputs["L0_ffn_ln_g"]),
        lnb=np.ascontiguousarray(inputs["L0_ffn_ln_b"]),
        qsks=qsks, **consts)
    nc = build_program(T=T, NL=NL, NE=NE)
    in_maps = []
    for ci in range(NCORE):
        xs_ = np.ascontiguousarray(
            x[ci * NE:(ci + 1) * NE].transpose(0, 2, 1))
        m = dict(common)
        m["x0"] = xs_
        in_maps.append(m)
    res = run_bass_kernel_spmd(nc, in_maps, core_ids=list(range(NCORE)))
    out = np.empty((16, T, D), dtype=np.float32)
    for ci in range(NCORE):
        y = res.results[ci]["yout"]
        out[ci * NE:(ci + 1) * NE] = y.transpose(0, 2, 1)
    return out
